# version 07
# baseline (speedup 1.0000x reference)
import contextlib
import numpy as np
import concourse.bass as bass
import concourse.mybir as mybir
from concourse.bass_utils import run_bass_kernel_spmd

F32 = mybir.dt.float32
BF16 = mybir.dt.bfloat16
AF = mybir.ActivationFunctionType
ALU = mybir.AluOpType
AX = mybir.AxisListType

H = 1024
NKC = H // 128
NH = 16
D = 64
C = 128
ST = 512
NCH = ST // C
I_FFN = 2048
NIC = I_FFN // 128
NE = 8
NEGBIG = -1.0e30


class Buf:
    __slots__ = ("w", "rs", "name")

    def __init__(self, name=""):
        self.w = None
        self.rs = []
        self.name = name


class Op:
    __slots__ = ("i", "eng", "fn", "deps", "queue", "need_inc", "val",
                 "waits", "pre_drain", "nodrain")

    def __init__(self, i, eng, fn, deps, queue):
        self.i = i
        self.eng = eng
        self.fn = fn
        self.deps = deps
        self.queue = queue
        self.need_inc = False
        self.val = None
        self.waits = []
        self.pre_drain = False
        self.nodrain = False


class Em:
    ENGS = ("T", "V", "S", "G", "Y")

    def __init__(self):
        self.all_ops = []
        self.queues = {}

    def op(self, eng, fn, reads=(), writes=(), extra=(), queue=None):
        deps = set()
        for b in reads:
            if b.w is not None:
                deps.add(b.w)
        for b in writes:
            if b.w is not None:
                deps.add(b.w)
            deps.update(b.rs)
        for t in extra:
            if t is not None:
                deps.add(t)
        o = Op(len(self.all_ops), eng, fn, sorted(deps), queue)
        self.all_ops.append(o)
        if queue is not None:
            self.queues.setdefault(queue, []).append(o.i)
        for b in writes:
            b.w = o.i
            b.rs = []
        for b in reads:
            b.rs.append(o.i)
        return o.i

    def dma(self, eng, queue, fn, reads=(), writes=(), extra=()):
        return self.op(eng, fn, reads=reads, writes=writes, extra=extra,
                       queue=queue)

    def finalize(self):
        ops = self.all_ops
        for o in ops:
            for d in o.deps:
                p = ops[d]
                if p.queue is None and p.eng == o.eng:
                    continue
                p.need_inc = True
        for q, ids in self.queues.items():
            for n, i in enumerate(ids):
                ops[i].need_inc = True
                ops[i].val = 16 * (n + 1)
        cnt = {e: 0 for e in self.ENGS}
        for o in ops:
            if o.queue is None and o.need_inc:
                cnt[o.eng] += 1
                o.val = cnt[o.eng]
        wm = {e: {} for e in self.ENGS}
        drain_cover = {e: -1 for e in self.ENGS}
        prev_op = {e: -1 for e in self.ENGS}
        for o in ops:
            need_drain = False
            cand = {}
            for d in o.deps:
                p = ops[d]
                if p.queue is None and p.eng == o.eng:
                    if d > drain_cover[o.eng]:
                        need_drain = True
                    continue
                key = ("q", p.queue) if p.queue is not None else ("e", p.eng)
                if p.val > cand.get(key, 0):
                    cand[key] = p.val
            for key, val in cand.items():
                if val <= wm[o.eng].get(key, 0):
                    continue
                wm[o.eng][key] = val
                o.waits.append((key, val))
            if need_drain:
                o.pre_drain = True
                drain_cover[o.eng] = prev_op[o.eng]
            if o.queue is None:
                prev_op[o.eng] = o.i
        return ops


def _tile_lhsT(w):
    O, Hin = w.shape
    wt = np.ascontiguousarray(w.T)
    a = wt.reshape(Hin // 128, 128, O // 128, 128)
    return np.ascontiguousarray(
        a.transpose(2, 1, 0, 3).reshape(O // 128, 128, Hin))


def _tile_small(w):
    O, Hin = w.shape
    wt = np.ascontiguousarray(w.T).reshape(Hin // 128, 128, O)
    return np.ascontiguousarray(
        wt.transpose(1, 0, 2).reshape(128, (Hin // 128) * O))


def _masks():
    r = np.arange(C)
    mb = lambda m: np.where(m, 0.0, NEGBIG).astype(np.float32)
    return (mb(r[:, None] > r[None, :]), mb(r[:, None] < r[None, :]),
            mb(r[:, None] <= r[None, :]))


def _headmask():
    hm = np.zeros((NKC, 128, NH), np.float32)
    for hc in range(NKC):
        hm[hc, 0:64, 2 * hc] = 1.0
        hm[hc, 64:128, 2 * hc + 1] = 1.0
    return np.ascontiguousarray(hm.transpose(1, 0, 2).reshape(128, NKC * NH))


def _emit(nc, em, ctx):
    import contextlib
    ops = em.finalize()
    sem_keys = set()
    for o in ops:
        for (key, _v) in o.waits:
            sem_keys.add(key)
        if o.need_inc:
            sem_keys.add(("q", o.queue) if o.queue is not None
                         else ("e", o.eng))
    semctx = contextlib.ExitStack()
    with semctx:
        sems = {}
        for i, key in enumerate(sorted(sem_keys, key=str)):
            sems[key] = semctx.enter_context(
                nc.semaphore(f"s{i}_{key[1]}"))
        per_eng = {e: [o for o in ops if o.eng == e] for e in Em.ENGS}

        def runner(eng_name):
            def go(eng):
                for o in per_eng[eng_name]:
                    if o.pre_drain:
                        eng.drain()
                    for (key, val) in o.waits:
                        eng.wait_ge(sems[key], val)
                    inst = o.fn(eng)
                    if o.need_inc:
                        key = (("q", o.queue) if o.queue is not None
                               else ("e", o.eng))
                        inst.then_inc(sems[key],
                                      16 if o.queue is not None else 1)
            return go

        with nc.Block() as block:
            block.tensor(runner("T"))
            block.vector(runner("V"))
            block.scalar(runner("S"))
            block.gpsimd(runner("G"))
            block.sync(runner("Y"))
    return nc


def build_nc(n_pre=4, n_main=4, skip_ffn=False, lim_chunks=None, skip_scan=False, lim_heads=None, skip_wo=False, hstages=('gd','eb','sq','tok','solve','wk','o','state'), dbg=False):
    TP = n_pre * ST
    TM = n_main * ST
    NTT = n_main
    nc = bass.Bass("TRN2", target_bir_lowering=False, debug=False)

    dr = {}
    def din(name, shape):
        dr[name] = nc.dram_tensor(name, list(shape), F32,
                                  kind="ExternalInput").ap()

    din("xmain", (H, TM))
    din("xpre", (H, TP))
    for nm in ("wq", "wk", "wv", "wgt", "wo"):
        din(nm, (NKC, 128, H))
    din("wgs", (NIC, 128, H)); din("wus", (NIC, 128, H))
    din("wds", (NKC, 128, I_FFN))
    din("wge", (NE * NIC, 128, H)); din("wue", (NE * NIC, 128, H))
    din("wde", (NE * NKC, 128, I_FFN))
    din("wbt", (128, NKC * NH)); din("wat", (128, NKC * NH))
    din("wrt", (128, NKC * NE))
    din("convq", (128, NKC * 4)); din("convk", (128, NKC * 4))
    din("convv", (128, NKC * 4))
    din("alog", (NH, 1)); din("dtb", (NH, 1)); din("normw2", (128, 1))
    din("rb8", (NE, 1))
    din("ident", (128, 128))
    din("mb_lo", (128, 2 * C)); din("mb_upi", (128, 2 * C))
    din("hmask", (128, NKC * NH))
    yout = nc.dram_tensor("y", [H, TM], F32, kind="ExternalOutput").ap()
    cwst = nc.dram_tensor("cwstash", [NE, TM], F32, kind="Internal").ap()

    em = Em()
    B = {}
    def buf(name):
        if name not in B:
            B[name] = Buf(name)
        return B[name]

    ctx = contextlib.ExitStack()
    sb = lambda name, shape: ctx.enter_context(
        nc.sbuf_tensor(name, list(shape), F32))
    ps = lambda name, shape: ctx.enter_context(
        nc.psum_tensor(name, list(shape), F32))

    with ctx:
        x_res = sb("x_res", (128, NKC * TM))
        AR_COLS = 25568
        arena = sb("arena", (128, AR_COLS))
        upslots = [sb("up0", (128, H)), sb("up1", (128, H))]
        dnslots = [sb("dn0", (128, H)), sb("dn1", (128, H))]
        c_wbt = sb("c_wbt", (128, NKC * NH))
        c_wat = sb("c_wat", (128, NKC * NH))
        c_wrt = sb("c_wrt", (128, NKC * NE))
        c_cv = {"q": sb("c_convq", (128, NKC * 4)),
                "k": sb("c_convk", (128, NKC * 4)),
                "v": sb("c_convv", (128, NKC * 4))}
        c_alog = sb("c_alog", (NH, 1)); c_dtb = sb("c_dtb", (NH, 1))
        c_negA = sb("c_negA", (NH, 1)); c_normw2 = sb("c_normw2", (128, 1))
        c_rb8 = sb("c_rb8", (NE, 1))
        c_id = sb("c_id", (128, 128))
        c_mblo = sb("c_mblo", (128, 2 * C))
        c_mbupi = sb("c_mbupi", (128, 2 * C))
        c_hmask = sb("c_hmask", (128, NKC * NH))
        c_ones = sb("c_ones", (128, 1)); c_onesrow = sb("c_onesrow", (33, 128))
        c_eps5a = sb("c_eps5a", (1, 1))
        c_eps6 = sb("c_eps6", (NH, 1))
        c_eps5b = sb("c_eps5b", (128, 1))
        hist = {"q": sb("histq", (128, NKC * 4)),
                "k": sb("histk", (128, NKC * 4)),
                "v": sb("histv", (128, NKC * 4))}
        sml = sb("sml", (NH, 4228))
        g16 = sml[:, 0:ST]; beta16 = sml[:, 512:512 + ST]
        lnb16 = sml[:, 1024:1024 + ST]; lk16 = sml[:, 1536:1536 + ST]
        G16 = sml[:, 2048:2048 + ST]
        RA16 = sml[:, 2560:2560 + ST]
        RB16 = sml[:, 3072:3072 + ST]
        rq16 = sml[:, 3584:3584 + ST]
        t16a = RA16; t16b = RB16
        e116 = sml[:, 4096:4096 + C]
        lamc16 = sml[:, 4224:4225]
        logit8 = sml[0:NE, 2560:2560 + ST]
        stp = sb("stp", (33, ST))
        stat_r = stp[0:1, 0:ST]; stat_m = stp[32:33, 0:ST]
        hscr = sb("hscr", (128, ST))
        stackA = sb("stackA", (128, C)); stackB = sb("stackB", (NH, C))
        tokcol = sb("tokcol", (128, 144))
        lamrep = sb("lamrep", (128, NH))
        lam1 = lamrep[0:1, :]
        RFP_o = 21472
        rfp = arena[0:34, RFP_o:RFP_o + 2 * NH * C]

        pb = [ps(f"pb{i}", (128, 512)) for i in range(8)]

        QO, KO, VO = 0, 4128, 8256
        XLN = 12384
        OFMC = 16480
        SCW = 17504
        PSL = [SCW, SCW + 512]
        E_o = SCW + 1024
        Bm_o = SCW + 1280
        BTm_o = SCW + 1536
        MTm_o = SCW + 1792
        UB_o = SCW + 2048
        U_o = SCW + 2304
        WKT_o = SCW + 2432
        KHAT_o = SCW + 2688
        OT_o = UB_o
        SQD_o = UB_o + 128
        MS_o = WKT_o
        S_ALL_o = SCW + 2816 + 128
        X2LN = 0
        HB = 16384
        HS_o = 20480
        SQ5_o = SCW

        def A(p0, p1, off, w):
            return arena[p0:p1, off:off + w]

        def qkv_ap(base, kc, a, b):
            return arena[:, base + kc * 516 + a: base + kc * 516 + b]

        def xln_ap(kc, a, b):
            return arena[:, XLN + kc * ST + a: XLN + kc * ST + b]

        def ofmc_ap(p0, p1, hc):
            return arena[p0:p1, OFMC + hc * C: OFMC + (hc + 1) * C]

        def x2ln_ap(kc, a, b):
            return arena[:, X2LN + kc * TM + a: X2LN + kc * TM + b]

        def h_ap(oc8, a, b):
            return arena[:, HB + oc8 * ST + a: HB + oc8 * ST + b]

        def xres_ap(kc, a, b):
            return x_res[:, kc * TM + a: kc * TM + b]

        def S_ap(p0, h):
            return arena[p0:p0 + 64, S_ALL_o + h * D: S_ALL_o + (h + 1) * D]

        bQ = [buf(f"q{k}") for k in range(NKC)]
        bK = [buf(f"k{k}") for k in range(NKC)]
        bV = [buf(f"v{k}") for k in range(NKC)]
        bXLN = [buf(f"xln{k}") for k in range(NKC)]
        bOFM = [buf(f"ofm{k}") for k in range(NKC)]
        bX2LN = [buf(f"x2ln{k}") for k in range(NKC)]
        bH = [buf(f"h{o}") for o in range(8)]
        bXR = [buf(f"xr{k}") for k in range(NKC)]
        bPB = [buf(f"pb{i}") for i in range(8)]
        bUP = [buf(f"up{i}") for i in range(2)]
        bDN = [buf(f"dn{i}") for i in range(2)]
        bPk = [buf(f"Pk{k}") for k in range(6)]
        bPTk = [buf(f"PTk{k}") for k in range(6)]
        for nm in ("consts g16 beta16 lnb16 lk16 rq16 G16 RA16 RB16 t16a "
                   "t16b e116 lamc lam1 stackA stackB tokcol lamrep S "
                   "sq512 cacc0 cacc1 statm statr statt cwrow eB eBT eM Bm "
                   "BTm MTm ub Ub wkT khat Otok sqd msr logit8 cwm cwmask "
                   "cwsig hsil cwst rfp hist xstage e2b lamb ps0 ps1").split():
            buf(nm)
        CONST = B["consts"]

        for nm, t in (("wbt", c_wbt), ("wat", c_wat), ("wrt", c_wrt),
                      ("convq", c_cv["q"]), ("convk", c_cv["k"]),
                      ("convv", c_cv["v"]), ("alog", c_alog),
                      ("dtb", c_dtb), ("normw2", c_normw2), ("rb8", c_rb8),
                      ("ident", c_id), ("mb_lo", c_mblo),
                      ("mb_upi", c_mbupi), ("hmask", c_hmask)):
            em.dma("Y", "qc", (lambda t=t, nm=nm: lambda e: e.dma_start(
                out=t[:], in_=dr[nm]))(), writes=[CONST])
        for kc in range(NKC):
            em.dma("Y", "qx", (lambda kc=kc: lambda e: e.dma_start(
                out=xres_ap(kc, 0, TM),
                in_=dr["xmain"][kc * 128:(kc + 1) * 128, :]))(),
                writes=[bXR[kc], B["xstage"]])

        def vset(ap, val, writes, reads=()):
            em.op("V", (lambda a=ap, v=val: lambda e: e.memset(a, v))(),
                  reads=list(reads), writes=list(writes))

        vset(c_ones[:], 1.0, [CONST])
        vset(c_onesrow[:], 1.0, [CONST])
        vset(c_eps5a[:], 1e-5, [CONST])
        vset(c_eps6[:], 1e-6, [CONST])
        vset(c_eps5b[:], 1e-5, [CONST])
        vset(arena[:, S_ALL_o:S_ALL_o + NH * D], 0.0, [B["S"]])
        vset(rfp[0:2, :], 1.0, [B["rfp"]])
        vset(rfp[32:34, :], 1.0, [B["rfp"]])
        vset(stackA[:], 0.0, [B["stackA"]])
        vset(stackB[:], 0.0, [B["stackB"]])
        for t in ("q", "k", "v"):
            vset(hist[t][:], 0.0, [B["hist"]])
        em.op("S", lambda e: e.activation(c_negA[:], c_alog[:], AF.Exp,
                                          bias=0.0, scale=1.0),
              reads=[CONST], writes=[CONST])
        em.op("V", lambda e: e.tensor_scalar_mul(c_negA[:], c_negA[:], -1.0),
              reads=[CONST], writes=[CONST])

        upq = [0]; dnq = [0]

        def next_up():
            i = upq[0] % 2; upq[0] += 1
            return upslots[i], bUP[i], f"qu{i}"

        def next_dn():
            i = dnq[0] % 2; dnq[0] += 1
            return dnslots[i], bDN[i], f"qd{i}"

        WPU = [(arena[:, 23520 + i * 1024:23520 + i * 1024 + 512
                       ].bitcast(BF16),
                arena[:, 23520 + i * 1024 + 512:23520 + (i + 1) * 1024
                      ].bitcast(BF16)) for i in range(2)]
        WPD = [(arena[:, RFP_o + i * 1024:RFP_o + i * 1024 + 512
                      ].bitcast(BF16),
                arena[:, RFP_o + i * 1024 + 512:RFP_o + (i + 1) * 1024
                      ].bitcast(BF16)) for i in range(2)]
        bWPU = [buf("wpu0"), buf("wpu1")]
        bWPD = [buf("wpd0"), buf("wpd1")]
        upwq = [0]; dnwq = [0]

        def next_upw():
            i = upwq[0] % 2; upwq[0] += 1
            return upslots[i], bUP[i], f"qu{i}", WPU[i], bWPU[i]

        def next_dnw():
            i = dnwq[0] % 2; dnwq[0] += 1
            return dnslots[i], bDN[i], f"qd{i}", WPD[i], bWPD[i]

        def xh_ap(kc, t0):
            o = X2LN + kc * (TM // 2) + t0 // 2
            return arena[:, o:o + ST // 2].bitcast(BF16)

        def xl_ap(kc, t0):
            o = X2LN + (NKC + kc) * (TM // 2) + t0 // 2
            return arena[:, o:o + ST // 2].bitcast(BF16)

        def hh_ap(oc8):
            o = HB + oc8 * ST
            return arena[:, o:o + ST // 2].bitcast(BF16)

        def hl_ap(oc8):
            o = HB + oc8 * ST + ST // 2
            return arena[:, o:o + ST // 2].bitcast(BF16)

        def ldsplit(queue, stage, stage_b, pair, pair_b, dram_ap):
            em.dma("Y", queue, (lambda t=stage, d=dram_ap:
                                lambda e: e.dma_start(
                out=t[:, 0:1024], in_=d))(), writes=[stage_b])
            wh, wl = pair
            vop((lambda wh=wh, t=stage: lambda e: e.tensor_copy(
                wh, t[:, 0:1024]))(),
                reads=[stage_b], writes=[pair_b])
            vop((lambda wh=wh, wl=wl, t=stage: lambda e: e.tensor_tensor(
                wl, t[:, 0:1024], wh, ALU.subtract))(),
                reads=[stage_b], writes=[pair_b])

        def ld(queue, slot_t, slot_b, dram_ap, width):
            em.dma("Y", queue, (lambda t=slot_t, d=dram_ap, w=width:
                                lambda e: e.dma_start(out=t[:, 0:w],
                                                      in_=d))(),
                   writes=[slot_b])

        def mm(out_ap, lhsT_ap, rhs_ap, reads, wbank, start, stop,
               tr=False):
            i = em.op("T", (lambda o=out_ap, l=lhsT_ap, r=rhs_ap,
                            s=start, p=stop, t=tr: lambda e: e.matmul(
                o, l, r, start=s, stop=p,
                is_transpose=(True if t else None)))(),
                reads=list(reads), writes=[wbank])
            if not start:
                em.all_ops[i].nodrain = True
            return i

        def vop(fn, reads=(), writes=()):
            em.op("V", fn, reads=list(reads), writes=list(writes))

        def sop(fn, reads=(), writes=()):
            em.op("S", fn, reads=list(reads), writes=list(writes))

        def ln_pass(src_fn, src_bufs, dst_fn, dst_bufs, hook=None):
            for kc in range(NKC):
                sop((lambda kc=kc: lambda e: e.activation(
                    A(0, 128, SQ5_o, ST), src_fn(kc), AF.Square, bias=0.0,
                    scale=1.0))(), reads=[src_bufs[kc]],
                    writes=[B["sq512"], B["ps0"], B["ps1"]])
                mm(pb[7][0:1, 0:ST], c_ones[:], src_fn(kc),
                   [CONST, src_bufs[kc]], bPB[7], kc == 0, kc == NKC - 1)
                mm(pb[6][0:1, 0:ST], c_ones[:], A(0, 128, SQ5_o, ST),
                   [CONST, B["sq512"]], bPB[6], kc == 0, kc == NKC - 1)
            vop(lambda e: e.tensor_scalar_mul(stat_m, pb[7][0:1, 0:ST],
                                              1.0 / H),
                reads=[bPB[7]], writes=[B["statm"]])
            vop(lambda e: e.scalar_tensor_tensor(
                stat_r, stat_m, -1.0, stat_m, ALU.mult, ALU.mult),
                reads=[B["statm"]], writes=[B["statr"]])
            vop(lambda e: e.scalar_tensor_tensor(
                stat_r, pb[6][0:1, 0:ST], 1.0 / H, stat_r, ALU.mult,
                ALU.add),
                reads=[bPB[6]], writes=[B["statr"]])
            sop(lambda e: e.activation(stat_r, stat_r, AF.Sqrt,
                                       bias=c_eps5a[:], scale=1.0),
                reads=[CONST], writes=[B["statr"]])
            vop(lambda e: e.reciprocal(stat_r, stat_r),
                reads=[], writes=[B["statr"]])
            mm(pb[7][:, 0:ST], c_onesrow[32:33, :], stat_m,
               [CONST, B["statm"]], bPB[7], True, True)
            mm(pb[6][:, 0:ST], c_onesrow[0:1, :], stat_r,
               [CONST, B["statr"]], bPB[6], True, True)
            for kc in range(NKC):
                vop((lambda kc=kc: lambda e: e.tensor_tensor(
                    dst_fn(kc), src_fn(kc), pb[7][:, 0:ST],
                    ALU.subtract))(),
                    reads=[src_bufs[kc], bPB[7]], writes=[dst_bufs[kc]])
                vop((lambda kc=kc: lambda e: e.tensor_tensor(
                    dst_fn(kc), dst_fn(kc), pb[6][:, 0:ST], ALU.mult))(),
                    reads=[bPB[6]], writes=[dst_bufs[kc]])
                if hook is not None:
                    hook(kc)

        def proj16(wt_const, width, bank_i, nrows):
            for kc in range(NKC):
                mm(pb[bank_i][0:nrows, 0:ST],
                   wt_const[:, kc * width:(kc + 1) * width],
                   xln_ap(kc, 0, ST), [CONST, bXLN[kc]], bPB[bank_i],
                   kc == 0, kc == NKC - 1)

        def softplus_into(dst_ap, dst_b, src_ap, src_b, negate_arg):
            if negate_arg:
                vop((lambda s=src_ap: lambda e: e.tensor_scalar_mul(
                    t16a, s, -1.0))(),
                    reads=[src_b], writes=[B["RA16"]])
            else:
                vop((lambda s=src_ap: lambda e: e.tensor_scalar(
                    t16a, s, c_dtb[:], None, ALU.add))(),
                    reads=[src_b, CONST], writes=[B["RA16"]])
            sop(lambda e: e.activation(t16b, t16a, AF.Abs, bias=0.0,
                                       scale=1.0),
                reads=[B["RA16"]], writes=[B["RB16"]])
            sop(lambda e: e.activation(t16b, t16b, AF.Exp, bias=0.0,
                                       scale=-1.0),
                reads=[], writes=[B["RB16"]])
            sop(lambda e: e.activation(t16b, t16b, AF.Ln, bias=1.0,
                                       scale=1.0),
                reads=[], writes=[B["RB16"]])
            vop(lambda e: e.tensor_scalar_max(t16a, t16a, 0.0),
                reads=[], writes=[B["RA16"]])
            vop((lambda d=dst_ap: lambda e: e.tensor_tensor(
                d, t16a, t16b, ALU.add))(),
                reads=[B["RA16"], B["RB16"]], writes=[dst_b])

        def headpair(cc, j, is_main):
            hc = j
            hA, hB = 2 * j, 2 * j + 1
            kT = {0: A(0, 64, KO + hc * 516 + 4 + cc, C),
                  64: A(64, 128, KO + hc * 516 + 4 + cc, C)}
            vT = {0: A(0, 64, VO + hc * 516 + 4 + cc, C),
                  64: A(64, 128, VO + hc * 516 + 4 + cc, C)}
            qT = {0: A(0, 64, QO + hc * 516 + 4 + cc, C),
                  64: A(64, 128, QO + hc * 516 + 4 + cc, C)}
            bk, bv, bq = bK[hc], bV[hc], bQ[hc]
            bS = B["S"]
            eAB = A(0, 128, E_o, 2 * C)
            eA = A(0, 128, E_o, C); eBb = A(0, 128, E_o + C, C)
            BmAB = A(0, 128, Bm_o, 2 * C)
            Bm = {0: A(0, 128, Bm_o, C), 64: A(0, 128, Bm_o + C, C)}
            BTmAB = A(0, 128, BTm_o, 2 * C)
            BTm = {0: A(0, 128, BTm_o, C), 64: A(0, 128, BTm_o + C, C)}
            MTm = {0: A(0, 128, MTm_o, C), 64: A(0, 128, MTm_o + C, C)}
            ub = A(0, 128, UB_o, 2 * C)
            ubh = {0: A(0, 128, UB_o, C), 64: A(0, 128, UB_o + C, C)}
            UAB = A(0, 128, U_o, 2 * D)
            Uh = {0: A(0, 128, U_o, D), 64: A(0, 128, U_o + D, D)}
            wkTh = {0: A(0, 64, WKT_o, C), 64: A(0, 64, WKT_o + C, C)}
            khath = {0: A(0, 128, KHAT_o, D), 64: A(0, 128, KHAT_o + D, D)}
            OtokAB = A(0, 128, OT_o, 2 * D)
            Otokh = {0: A(0, 128, OT_o, D), 64: A(0, 128, OT_o + D, D)}
            PS = [A(0, 128, PSL[0], 512), A(0, 128, PSL[1], 512)]
            bPS = [B["ps0"], B["ps1"]]
            ro = {0: hA * C, 64: hB * C}
            RO2 = NH * C
            bcol = lambda q, h: tokcol[:, 32 * q + h:32 * q + h + 1]

            mm(pb[0][:, 0:C], kT[0], kT[0], [bk], bPB[0], True, False)
            mm(pb[0][:, C:2 * C], kT[64], kT[64], [bk], bPB[0], False,
               not is_main)
            if is_main:
                mm(pb[0][:, 2 * C:3 * C], kT[0], qT[0], [bk, bq], bPB[0],
                   False, False)
                mm(pb[0][:, 3 * C:4 * C], kT[64], qT[64], [bk, bq], bPB[0],
                   False, True)
            mm(pb[1][:, 0:C], rfp[0:2, ro[0]:ro[0] + C],
               rfp[0:2, RO2 + ro[0]:RO2 + ro[0] + C], [B["rfp"]], bPB[1],
               True, False)
            mm(pb[1][:, C:2 * C], rfp[0:2, ro[64]:ro[64] + C],
               rfp[0:2, RO2 + ro[64]:RO2 + ro[64] + C], [B["rfp"]],
               bPB[1], False, not is_main)
            if is_main:
                mm(pb[1][:, 2 * C:3 * C], rfp[32:34, ro[0]:ro[0] + C],
                   rfp[32:34, RO2 + ro[0]:RO2 + ro[0] + C], [B["rfp"]],
                   bPB[1], False, False)
                mm(pb[1][:, 3 * C:4 * C], rfp[32:34, ro[64]:ro[64] + C],
                   rfp[32:34, RO2 + ro[64]:RO2 + ro[64] + C], [B["rfp"]],
                   bPB[1], False, True)
            vop(lambda e: e.tensor_tensor(eAB, pb[1][:, 0:2 * C],
                                          c_mblo[:], ALU.add),
                reads=[bPB[1], CONST], writes=[B["eB"]])
            sop(lambda e: e.activation(eAB, eAB, AF.Exp, bias=0.0,
                                       scale=1.0),
                reads=[], writes=[B["eB"]])
            vop(lambda e: e.tensor_tensor(BmAB, pb[0][:, 0:2 * C], eAB,
                                          ALU.mult),
                reads=[bPB[0], B["eB"]], writes=[B["Bm"]])
            if is_main:
                vop(lambda e: e.tensor_tensor(eAB, pb[1][:, 2 * C:4 * C],
                                              c_mbupi[:], ALU.add),
                    reads=[bPB[1], CONST], writes=[B["eB"]])
                sop(lambda e: e.activation(eAB, eAB, AF.Exp, bias=0.0,
                                           scale=1.0),
                    reads=[], writes=[B["eB"]])
                vop(lambda e: e.tensor_tensor(
                    A(0, 128, MTm_o, 2 * C), pb[0][:, 2 * C:4 * C], eAB,
                    ALU.mult),
                    reads=[bPB[0], B["eB"]], writes=[B["MTm"]])
            mm(pb[1][:, 0:C], Bm[0], c_id[:], [B["Bm"], CONST], bPB[1],
               True, False, tr=True)
            mm(pb[1][:, C:2 * C], Bm[64], c_id[:], [B["Bm"], CONST],
               bPB[1], False, True, tr=True)
            vop(lambda e: e.tensor_copy(BTmAB, pb[1][:, 0:2 * C]),
                reads=[bPB[1]], writes=[B["BTm"]])
            mm(pb[4][:, 0:D], vT[0], c_id[0:64, 0:64], [bv, CONST],
               bPB[4], True, False, tr=True)
            mm(pb[4][:, D:2 * D], kT[0], c_id[0:64, 0:64], [bk, CONST],
               bPB[4], False, False, tr=True)
            mm(pb[4][:, 2 * D:3 * D], vT[64], c_id[64:128, 64:128],
               [bv, CONST], bPB[4], False, False, tr=True)
            mm(pb[4][:, 3 * D:4 * D], kT[64], c_id[64:128, 64:128],
               [bk, CONST], bPB[4], False, True, tr=True)
            for (p0, h, o4) in ((0, hA, 0), (64, hB, 2)):
                vop((lambda p0=p0, h=h, o4=o4: lambda e: e.tensor_scalar(
                    ub[:, o4 * D:(o4 + 1) * D], pb[4][:, o4 * D:(o4 + 1) * D],
                    bcol(0, h), None, ALU.mult))(),
                    reads=[bPB[4], B["tokcol"]], writes=[B["ub"]])
                vop((lambda p0=p0, h=h, o4=o4: lambda e: e.tensor_scalar(
                    ub[:, (o4 + 1) * D:(o4 + 2) * D],
                    pb[4][:, (o4 + 1) * D:(o4 + 2) * D],
                    bcol(2, h), None, ALU.mult))(),
                    reads=[bPB[4], B["tokcol"]], writes=[B["ub"]])
                vop((lambda p0=p0, h=h, o4=o4: lambda e: e.tensor_scalar(
                    khath[p0], pb[4][:, (o4 + 1) * D:(o4 + 2) * D],
                    bcol(1, h), None, ALU.mult))(),
                    reads=[bPB[4], B["tokcol"]], writes=[B["khat"]])
            mm(pb[5][:, 0:C], BTm[0], ubh[0], [B["BTm"], B["ub"]], bPB[5],
               True, False)
            mm(pb[5][:, C:2 * C], BTm[64], ubh[64], [B["BTm"], B["ub"]],
               bPB[5], False, True)
            vop(lambda e: e.scalar_tensor_tensor(
                ub, pb[5][:, 0:2 * C], -1.0, ub, ALU.mult, ALU.add),
                reads=[bPB[5]], writes=[B["ub"]])
            for L in range(1, 7):
                sl_i = L % 2
                bank = bPB[2 + sl_i]
                pbank = pb[2 + sl_i]
                if L == 1:
                    lA, rA = BTm[0], Bm[0]
                    lB, rB = BTm[64], Bm[64]
                    rdep = [B["Bm"], B["BTm"]]
                else:
                    prev = PS[(L - 1) % 2]
                    lA = A(0, 128, PSL[(L - 1) % 2] + C, C)
                    rA = A(0, 128, PSL[(L - 1) % 2], C)
                    lB = A(0, 128, PSL[(L - 1) % 2] + 3 * C, C)
                    rB = A(0, 128, PSL[(L - 1) % 2] + 2 * C, C)
                    rdep = [bPS[(L - 1) % 2]]
                mm(pbank[:, 0:C], lA, rA, rdep, bank, True, False)
                mm(pbank[:, C:2 * C], rA, lA, rdep, bank, False, False)
                mm(pbank[:, 2 * C:3 * C], lB, rB, rdep, bank, False, False)
                mm(pbank[:, 3 * C:4 * C], rB, lB, rdep, bank, False, True)
                vop((lambda sl_i=sl_i, pbank=pbank: lambda e: e.tensor_copy(
                    PS[sl_i], pbank[:]))(),
                    reads=[bank],
                    writes=[bPS[sl_i]] + ([B["sq512"]] if sl_i == 0
                                          else []))
                mm(pb[5][:, 0:C], A(0, 128, PSL[sl_i] + C, C), ubh[0],
                   [bPS[sl_i], B["ub"]], bPB[5], True, False)
                mm(pb[5][:, C:2 * C], A(0, 128, PSL[sl_i] + 3 * C, C),
                   ubh[64], [bPS[sl_i], B["ub"]], bPB[5], False, True)
                vop(lambda e: e.tensor_tensor(ub, ub, pb[5][:, 0:2 * C],
                                              ALU.add),
                    reads=[bPB[5]], writes=[B["ub"]])
            mm(pb[6][0:64, 0:C], ub[:, D:2 * D], c_id[:],
               [B["ub"], CONST], bPB[6], True, False, tr=True)
            mm(pb[6][0:64, C:2 * C], ub[:, 3 * D:4 * D], c_id[:],
               [B["ub"], CONST], bPB[6], False, True, tr=True)
            vop(lambda e: e.tensor_copy(A(0, 64, WKT_o, 2 * C),
                                        pb[6][0:64, 0:2 * C]),
                reads=[bPB[6]], writes=[B["wkT"]])
            mm(pb[6][:, 2 * C:2 * C + D], wkTh[0], S_ap(0, hA),
               [B["wkT"], bS], bPB[6], True, False)
            mm(pb[6][:, 2 * C + D:2 * C + 2 * D], wkTh[64], S_ap(0, hB),
               [B["wkT"], bS], bPB[6], False, True)
            vop(lambda e: e.scalar_tensor_tensor(
                UAB, pb[6][:, 2 * C:2 * C + 2 * D], -1.0,
                ub.rearrange("p (h w) -> p h w", h=2)[:, :, 0:D],
                ALU.mult, ALU.add),
                reads=[bPB[6], B["ub"]], writes=[B["Ub"]])
            if is_main:
                mm(pb[7][:, 0:D], MTm[0], Uh[0], [B["MTm"], B["Ub"]],
                   bPB[7], True, False)
                mm(pb[7][:, D:2 * D], qT[0], S_ap(0, hA), [bq, bS],
                   bPB[7], False, False)
                mm(pb[7][:, 2 * D:3 * D], MTm[64], Uh[64],
                   [B["MTm"], B["Ub"]], bPB[7], False, False)
                mm(pb[7][:, 3 * D:4 * D], qT[64], S_ap(64, hB), [bq, bS],
                   bPB[7], False, True)
                for (p0, h, o4) in ((0, hA, 0), (64, hB, 2)):
                    vop((lambda p0=p0, h=h, o4=o4:
                         lambda e: e.tensor_copy(
                        Otokh[p0], pb[7][:, o4 * D:(o4 + 1) * D]))(),
                        reads=[bPB[7]], writes=[B["ub"]])
                    vop((lambda p0=p0, h=h, o4=o4:
                         lambda e: e.scalar_tensor_tensor(
                        Otokh[p0], pb[7][:, (o4 + 1) * D:(o4 + 2) * D],
                        bcol(3, h), Otokh[p0], ALU.mult, ALU.add))(),
                        reads=[bPB[7], B["tokcol"]], writes=[B["ub"]])
                    vop((lambda p0=p0, h=h: lambda e: e.tensor_scalar(
                        Otokh[p0], Otokh[p0], tokcol[:, 128 + h:129 + h],
                        None, ALU.mult))(),
                        reads=[B["tokcol"]], writes=[B["ub"]])
                sop(lambda e: e.activation(
                    A(0, 128, SQD_o, D), Otokh[0], AF.Square, bias=0.0,
                    scale=1.0, accum_out=A(0, 128, MS_o, 1)),
                    reads=[B["ub"]], writes=[B["ub"], B["wkT"]])
                sop(lambda e: e.activation(
                    A(0, 128, SQD_o + D, D), Otokh[64], AF.Square,
                    bias=0.0, scale=1.0, accum_out=A(0, 128, MS_o + 1, 1)),
                    reads=[B["ub"]], writes=[B["ub"], B["wkT"]])
                sop(lambda e: e.activation(
                    A(0, 128, MS_o + 2, 2), A(0, 128, MS_o, 2), AF.Sqrt,
                    bias=c_eps5b[:], scale=1.0 / D),
                    reads=[B["wkT"], CONST], writes=[B["wkT"]])
                vop(lambda e: e.reciprocal(A(0, 128, MS_o + 4, 2),
                                           A(0, 128, MS_o + 2, 2)),
                    reads=[], writes=[B["wkT"]])
                vop(lambda e: e.tensor_scalar(
                    Otokh[0], Otokh[0], A(0, 128, MS_o + 4, 1), None,
                    ALU.mult),
                    reads=[B["wkT"]], writes=[B["ub"]])
                vop(lambda e: e.tensor_scalar(
                    Otokh[64], Otokh[64], A(0, 128, MS_o + 5, 1), None,
                    ALU.mult),
                    reads=[B["wkT"]], writes=[B["ub"]])
                mm(pb[6][0:64, 0:C], Otokh[0], c_id[:],
                   [B["ub"], CONST], bPB[6], True, False, tr=True)
                mm(pb[6][0:64, C:2 * C], Otokh[64], c_id[:],
                   [B["ub"], CONST], bPB[6], False, True, tr=True)
                vop((lambda hc=hc: lambda e: e.tensor_scalar(
                    ofmc_ap(0, 64, hc), pb[6][0:64, 0:C],
                    c_normw2[0:64, :], None, ALU.mult))(),
                    reads=[bPB[6], CONST], writes=[bOFM[hc]])
                vop((lambda hc=hc: lambda e: e.tensor_scalar(
                    ofmc_ap(64, 128, hc), pb[6][0:64, C:2 * C],
                    c_normw2[64:128, :], None, ALU.mult))(),
                    reads=[bPB[6], CONST], writes=[bOFM[hc]])
            mm(pb[4][0:64, 256:256 + D], khath[0], Uh[0],
               [B["khat"], B["Ub"]], bPB[4], True, False)
            mm(pb[4][0:64, 256 + D:256 + 2 * D], khath[64], Uh[64],
               [B["khat"], B["Ub"]], bPB[4], False, True)
            mm(pb[4][64:128, 256 + 2 * D:256 + 3 * D], khath[0], Uh[0],
               [B["khat"], B["Ub"]], bPB[4], True, False)
            mm(pb[4][64:128, 256 + 3 * D:256 + 4 * D], khath[64], Uh[64],
               [B["khat"], B["Ub"]], bPB[4], False, True)
            for (h, col, prow) in ((hA, 256, 0), (hB, 256 + D, 0),
                                   (hA, 256 + 2 * D, 64),
                                   (hB, 256 + 3 * D, 64)):
                em.op("V", (lambda h=h, col=col, prow=prow:
                            lambda e: e.scalar_tensor_tensor(
                    S_ap(prow, h), S_ap(prow, h),
                    lamrep[prow:prow + 64, h:h + 1],
                    pb[4][prow:prow + 64, col:col + D], ALU.mult,
                    ALU.add))(),
                    reads=[bPB[4], B["lamrep"]], writes=[bS])

        def chunk(t0, cc, is_main):
            sl = slice(cc, cc + C)
            e2b = g16[:, sl]
            lamb = lnb16[:, sl]
            sop((lambda cc=cc: lambda e: e.activation(
                lamc16, G16[:, cc + C - 1:cc + C], AF.Exp, bias=0.0,
                scale=1.0))(), reads=[B["G16"]], writes=[B["lamc"]])
            vop((lambda sl=sl, cc=cc: lambda e: e.tensor_scalar(
                e116, RB16[:, sl], G16[:, cc + C - 1:cc + C], None,
                ALU.add))(), reads=[B["RB16"], B["G16"]],
                writes=[B["e116"]])
            sop(lambda e: e.activation(e116, e116, AF.Exp, bias=0.0,
                                       scale=1.0),
                reads=[], writes=[B["e116"]])
            sop((lambda sl=sl, e2b=e2b: lambda e: e.activation(
                e2b, RA16[:, sl], AF.Exp, bias=0.0, scale=1.0))(),
                reads=[B["RA16"]], writes=[B["g16"]])
            sop((lambda sl=sl, lamb=lamb: lambda e: e.activation(
                lamb, G16[:, sl], AF.Exp, bias=0.0, scale=1.0))(),
                reads=[B["G16"]], writes=[B["lnb16"]])
            def skdma(dst_p, src_ap, src_buf, dstt=None):
                dt = stackA if dstt is None else dstt
                em.dma("G", "qsk" if dstt is None else "qsk2",
                       (lambda dp=dst_p, s=src_ap, dt=dt:
                        lambda e: e.dma_start(
                    out=dt[dp:dp + NH, :], in_=s))(),
                    reads=[src_buf],
                    writes=[B["stackA"] if dstt is None else B["stackB"]])
            skdma(0, beta16[:, sl], B["beta16"])
            skdma(32, e116, B["e116"])
            skdma(64, e2b, B["g16"])
            skdma(96, lamb, B["lnb16"])
            if is_main:
                skdma(0, rq16[:, sl], B["rq16"], dstt=stackB)
            mm(pb[6][:, 0:128], stackA[:], c_id[:], [B["stackA"], CONST],
               bPB[6], True, not is_main, tr=True)
            if is_main:
                mm(pb[6][:, 128:128 + NH], stackB[0:NH, :],
                   c_id[0:NH, 0:NH], [B["stackB"], CONST], bPB[6],
                   False, True, tr=True)
            vop(lambda e: e.tensor_copy(tokcol[:, 0:128],
                                        pb[6][:, 0:128]),
                reads=[bPB[6]], writes=[B["tokcol"]])
            if is_main:
                vop(lambda e: e.tensor_copy(tokcol[:, 128:144],
                                            pb[6][:, 128:128 + NH]),
                    reads=[bPB[6]], writes=[B["tokcol"]])
            mm(pb[7][0:1, 0:NH], lamc16, c_id[0:NH, 0:NH],
               [B["lamc"], CONST], bPB[7], True, True, tr=True)
            vop(lambda e: e.tensor_copy(lam1, pb[7][0:1, 0:NH]),
                reads=[bPB[7]], writes=[B["lamrep"]])
            mm(pb[7][:, 64:64 + NH], c_onesrow[0:1, :], lam1,
               [CONST, B["lamrep"]], bPB[7], True, True)
            vop(lambda e: e.tensor_copy(lamrep[:], pb[7][:, 64:64 + NH]),
                reads=[bPB[7]], writes=[B["lamrep"]])

            def rfdma(dst_row, dst_off, src_ap, src_buf):
                em.dma("G", "qrf", (lambda r=dst_row, o=dst_off, s=src_ap:
                                    lambda e: e.dma_start(
                    out=rfp[r:r + 1, o:o + NH * C].rearrange(
                        "o (h t) -> o h t", h=NH),
                    in_=s))(), reads=[src_buf], writes=[B["rfp"]])
            RO2 = NH * C
            rfdma(0, 0, RA16[:, sl], B["RA16"])
            rfdma(1, RO2, RB16[:, sl], B["RB16"])
            rfdma(32, 0, RB16[:, sl], B["RB16"])
            rfdma(33, RO2, G16[:, sl], B["G16"])

            npair = 8 if lim_heads is None else max(1, lim_heads // 2)
            for j in range(npair):
                headpair(cc, j, is_main)

            if is_main and skip_wo:
                return
            if is_main:
                for hc in range(NKC):
                    vop((lambda hc=hc, cc=cc: lambda e: e.tensor_tensor(
                        ofmc_ap(0, 128, hc), ofmc_ap(0, 128, hc),
                        xln_ap(hc, cc, cc + C), ALU.mult))(),
                        reads=[bXLN[hc]], writes=[bOFM[hc]])
                for oc in range(NKC):
                    slot, sb_, qn = next_up()
                    ld(qn, slot, sb_, dr["wo"][oc], H)
                    bank = bPB[4 + (oc % 2)]
                    pbank = pb[4 + (oc % 2)]
                    qtr = (oc // 2) % 2
                    for kc in range(NKC):
                        mm(pbank[:, qtr * C:(qtr + 1) * C],
                           slot[:, kc * 128:(kc + 1) * 128],
                           ofmc_ap(0, 128, kc), [sb_, bOFM[kc]], bank,
                           kc == 0, kc == NKC - 1)
                    vop((lambda oc=oc, t0=t0, cc=cc, pbank=pbank,
                         qtr=qtr: lambda e: e.tensor_tensor(
                        xres_ap(oc, t0 + cc, t0 + cc + C),
                        xres_ap(oc, t0 + cc, t0 + cc + C),
                        pbank[:, qtr * C:(qtr + 1) * C], ALU.add))(),
                        reads=[bank], writes=[bXR[oc]])

        def supertile(sti, is_main, qproj):
            t0 = sti * ST
            if is_main:
                src_fn = lambda kc: xres_ap(kc, t0, t0 + ST)
                src_bufs = bXR
            else:
                for kc in range(NKC):
                    em.dma("Y", "qx", (lambda kc=kc, t0=t0:
                                       lambda e: e.dma_start(
                        out=xln_ap(kc, 0, ST),
                        in_=dr["xpre"][kc * 128:(kc + 1) * 128,
                                       t0:t0 + ST]))(),
                        writes=[bXLN[kc], B["xstage"]])
                src_fn = lambda kc: xln_ap(kc, 0, ST)
                src_bufs = bXLN
            ln_pass(src_fn, src_bufs, lambda kc: xln_ap(kc, 0, ST), bXLN)
            proj16(c_wbt, NH, 0, NH)
            sop(lambda e: e.activation(beta16, pb[0][0:NH, 0:ST],
                                       AF.Sigmoid, bias=0.0, scale=1.0),
                reads=[bPB[0]], writes=[B["beta16"]])
            softplus_into(lnb16, B["lnb16"], pb[0][0:NH, 0:ST], bPB[0],
                          negate_arg=True)
            vop(lambda e: e.tensor_scalar_mul(lnb16, lnb16, -1.0),
                reads=[], writes=[B["lnb16"]])
            proj16(c_wat, NH, 1, NH)
            softplus_into(g16, B["g16"], pb[1][0:NH, 0:ST], bPB[1],
                          negate_arg=False)
            vop(lambda e: e.tensor_scalar(g16, g16, c_negA[:], None,
                                          ALU.mult),
                reads=[CONST], writes=[B["g16"]])
            for c in range(NCH):
                vop((lambda c=c: lambda e: e.tensor_tensor_scan(
                    G16[:, c * C:(c + 1) * C], g16[:, c * C:(c + 1) * C],
                    g16[:, c * C:(c + 1) * C], 0.0, ALU.add,
                    ALU.bypass))(),
                    reads=[B["g16"]], writes=[B["G16"]])
            tensors = [("k", "wk", KO, bK), ("v", "wv", VO, bV)]
            if qproj:
                tensors.append(("q", "wq", QO, bQ))
            for (tn, wnm, base, bufs) in tensors:
                vop((lambda tn=tn, base=base: lambda e: e.tensor_copy(
                    arena[:, base:base + NKC * 516].rearrange(
                        "p (kc c) -> p kc c", c=516)[:, :, 0:4],
                    hist[tn][:].rearrange("p (kc c) -> p kc c", c=4)))(),
                    reads=[B["hist"]],
                    writes=[bufs[k] for k in range(NKC)])
                for oc in range(NKC):
                    slot, sb_, qn = next_up()
                    ld(qn, slot, sb_, dr[wnm][oc], H)
                    bank = bPB[2 + (oc % 2)]
                    pbank = pb[2 + (oc % 2)]
                    for kc in range(NKC):
                        mm(pbank[:, 0:ST],
                           slot[:, kc * 128:(kc + 1) * 128],
                           xln_ap(kc, 0, ST), [sb_, bXLN[kc]], bank,
                           kc == 0, kc == NKC - 1)
                    vop((lambda oc=oc, base=base, pbank=pbank:
                         lambda e: e.tensor_copy(
                        qkv_ap(base, oc, 4, 516), pbank[:, 0:ST]))(),
                        reads=[bank], writes=[bufs[oc]])
                    vop((lambda oc=oc, base=base, tn=tn:
                         lambda e: e.tensor_copy(
                        hist[tn][:, oc * 4:(oc + 1) * 4],
                        qkv_ap(base, oc, 512, 516)))(),
                        reads=[bufs[oc]], writes=[B["hist"]])
                for oc in range(NKC):
                    acc = pb[6 + (oc % 2)][:, 0:ST]
                    acb = bPB[6 + (oc % 2)]
                    cw = c_cv[tn]
                    for j in range(4):
                        eng = "V"
                        if j == 0:
                            em.op(eng, (lambda oc=oc, base=base, cw=cw,
                                        acc=acc:
                                        lambda e: e.tensor_scalar(
                                acc, qkv_ap(base, oc, 1, 513),
                                cw[:, oc * 4:oc * 4 + 1], None,
                                ALU.mult))(),
                                reads=[bufs[oc], CONST], writes=[acb])
                        else:
                            em.op(eng, (lambda oc=oc, base=base, cw=cw,
                                        j=j, acc=acc:
                                        lambda e:
                                        e.scalar_tensor_tensor(
                                acc, qkv_ap(base, oc, 1 + j, 513 + j),
                                cw[:, oc * 4 + j:oc * 4 + j + 1], acc,
                                ALU.mult, ALU.add))(),
                                reads=[bufs[oc], CONST], writes=[acb])
                    sop((lambda oc=oc, base=base, acc=acc:
                         lambda e: e.activation(
                        qkv_ap(base, oc, 4, 516), acc, AF.Silu,
                        bias=0.0, scale=1.0))(),
                        reads=[acb], writes=[bufs[oc]])
            for hc in range(NKC):
                sop((lambda hc=hc: lambda e: e.activation(
                    A(0, 128, SQ5_o, ST), qkv_ap(KO, hc, 4, 516),
                    AF.Square, bias=0.0, scale=1.0))(),
                    reads=[bK[hc]],
                    writes=[B["sq512"], B["ps0"], B["ps1"]])
                mm(pb[4][0:NH, 0:ST], c_hmask[:, hc * NH:(hc + 1) * NH],
                   A(0, 128, SQ5_o, ST), [CONST, B["sq512"]], bPB[4],
                   hc == 0, hc == NKC - 1)
            sop(lambda e: e.activation(lk16, pb[4][0:NH, 0:ST], AF.Ln,
                                       bias=c_eps6[:], scale=1.0),
                reads=[bPB[4], CONST], writes=[B["lk16"]])
            vop(lambda e: e.tensor_scalar_mul(lk16, lk16, -0.5),
                reads=[], writes=[B["lk16"]])
            if is_main:
                for hc in range(NKC):
                    sop((lambda hc=hc: lambda e: e.activation(
                        A(0, 128, SQ5_o, ST), qkv_ap(QO, hc, 4, 516),
                        AF.Square, bias=0.0, scale=1.0))(),
                        reads=[bQ[hc]],
                        writes=[B["sq512"], B["ps0"], B["ps1"]])
                    mm(pb[5][0:NH, 0:ST],
                       c_hmask[:, hc * NH:(hc + 1) * NH],
                       A(0, 128, SQ5_o, ST), [CONST, B["sq512"]],
                       bPB[5], hc == 0, hc == NKC - 1)
                sop(lambda e: e.activation(rq16, pb[5][0:NH, 0:ST],
                                           AF.Sqrt, bias=c_eps6[:],
                                           scale=1.0),
                    reads=[bPB[5], CONST], writes=[B["rq16"]])
                vop(lambda e: e.reciprocal(rq16, rq16),
                    reads=[], writes=[B["rq16"]])
                vop(lambda e: e.tensor_scalar_mul(rq16, rq16, 0.125),
                    reads=[], writes=[B["rq16"]])
            vop(lambda e: e.tensor_tensor(RA16, lk16, lnb16, ALU.add),
                reads=[B["lk16"], B["lnb16"]], writes=[B["RA16"]])
            vop(lambda e: e.tensor_tensor(RA16, RA16, G16, ALU.add),
                reads=[B["G16"]], writes=[B["RA16"]])
            vop(lambda e: e.tensor_tensor(RB16, lk16, G16, ALU.subtract),
                reads=[B["lk16"], B["G16"]], writes=[B["RB16"]])
            if is_main:
                for oc in range(NKC):
                    slot, sb_, qn = next_up()
                    ld(qn, slot, sb_, dr["wgt"][oc], H)
                    for kc in range(NKC):
                        mm(pb[oc][:, 0:ST],
                           slot[:, kc * 128:(kc + 1) * 128],
                           xln_ap(kc, 0, ST), [sb_, bXLN[kc]], bPB[oc],
                           kc == 0, kc == NKC - 1)
                for oc in range(NKC):
                    sop((lambda oc=oc: lambda e: e.activation(
                        xln_ap(oc, 0, ST), pb[oc][:, 0:ST], AF.Silu,
                        bias=0.0, scale=1.0))(),
                        reads=[bPB[oc]], writes=[bXLN[oc]])
            nch = NCH if lim_chunks is None else min(NCH, lim_chunks)
            if not skip_scan:
                for c in range(nch):
                    chunk(t0, c * C, is_main)

        for sp in range(n_pre):
            supertile(sp, False, qproj=(sp == n_pre - 1))
        for sm_i in range(n_main):
            supertile(sm_i, True, qproj=True)

        if skip_ffn:
            st_toks = []
            if dbg:
                allb = [b for b in B.values()]
                dumps = [
                    (yout[0:128, 0:128], A(0, 128, Bm_o, C)),
                    (yout[0:128, 128:256], A(0, 128, E_o, C)),
                    (yout[0:128, 256:384], A(0, 128, MTm_o, C)),
                    (yout[128:130, 0:512], rfp[0:2, 0:512]),
                    (yout[132:134, 0:512], rfp[32:34, 0:512]),
                    (yout[162:164, 0:512], rfp[0:2, 2048:2560]),
                    (yout[166:168, 0:512], rfp[32:34, 2048:2560]),
                    (yout[196:324, 0:144], tokcol[:]),
                    (yout[324:452, 0:128], stackA[:]),
                    (yout[452:468, 0:512], sml[:, 2560:2560 + 512]),
                    (yout[468:484, 0:512], sml[:, 3072:3072 + 512]),
                    (yout[484:500, 0:512], sml[:, 2048:2048 + 512]),
                    (yout[500:516, 0:512], beta16),
                    (yout[516:644, 0:512], arena[:, KO:KO + 512]),
                    (yout[644:772, 0:512], arena[:, QO:QO + 512]),
                ]
                for (dst, src) in dumps:
                    st_toks.append(em.dma("Y", "qout",
                                          (lambda d=dst, sr=src:
                                           lambda e: e.dma_start(
                        out=d, in_=sr))(), reads=allb))
            else:
                for kc in range(NKC):
                    st_toks.append(em.dma("Y", "qout",
                                          (lambda kc=kc: lambda e: e.dma_start(
                        out=yout[kc * 128:(kc + 1) * 128, :],
                        in_=xres_ap(kc, 0, TM)))(), reads=[bXR[kc]]))
            em.op("Y", lambda e: e.nop(), extra=st_toks)
            return _emit(nc, em, ctx)
        attn_only = ([bQ[k] for k in range(NKC)]
                     + [bK[k] for k in range(NKC)]
                     + [bV[k] for k in range(NKC)] + bXLN + bOFM
                     + [B[n] for n in ("ub", "Ub", "tokcol", "eB", "eBT",
                                       "eM", "Bm", "BTm", "MTm", "wkT",
                                       "khat", "Otok", "sqd", "sq512",
                                       "cacc0", "cacc1", "msr", "S",
                                       "stackA", "stackB", "lamrep",
                                       "lamc", "lam1", "e116", "rfp",
                                       "g16", "beta16", "lnb16", "lk16",
                                       "G16", "RA16", "RB16", "rq16",
                                       "t16a", "t16b", "xstage", "hist",
                                       "e2b", "lamb")]
                     + [bPk[k] for k in range(6)]
                     + [bPTk[k] for k in range(6)])
        em.op("V", lambda e: e.engine_nop(), reads=attn_only,
              writes=(bX2LN + bH + bWPU + bWPD
                      + [B[n] for n in ("hsil", "cwst", "cwrow",
                                        "logit8", "cwm", "cwmask",
                                        "cwsig")]
                      + [buf("hscr"), buf("cwb")]))
        scr_ap = lambda kc: (hscr[:, 0:ST] if kc % 2 == 0
                             else A(0, 128, HS_o, ST))
        scr_b = lambda kc: (B["hscr"] if kc % 2 == 0 else B["hsil"])
        for tt in range(NTT):
            t0 = tt * ST

            def _hook(kc, t0=t0):
                mm(pb[0][0:NE, 0:ST], c_wrt[:, kc * NE:(kc + 1) * NE],
                   scr_ap(kc), [CONST, scr_b(kc)], bPB[0],
                   kc == 0, kc == NKC - 1)
                vop((lambda kc=kc, t0=t0: lambda e: e.tensor_copy(
                    xh_ap(kc, t0), scr_ap(kc)))(),
                    reads=[scr_b(kc)], writes=[bX2LN[kc]])
                vop((lambda kc=kc, t0=t0: lambda e: e.tensor_tensor(
                    xl_ap(kc, t0), scr_ap(kc), xh_ap(kc, t0),
                    ALU.subtract))(),
                    reads=[scr_b(kc)], writes=[bX2LN[kc]])

            ln_pass(lambda kc, t0=t0: xres_ap(kc, t0, t0 + ST), bXR,
                    lambda kc: scr_ap(kc),
                    [scr_b(kc) for kc in range(NKC)], hook=_hook)
            vop(lambda e: e.tensor_scalar(logit8, pb[0][0:NE, 0:ST],
                                          c_rb8[:], None, ALU.add),
                reads=[bPB[0], CONST], writes=[B["logit8"]])
            for s4 in range(4):
                mm(pb[1][:, s4 * NE:(s4 + 1) * NE],
                   logit8[:, s4 * C:(s4 + 1) * C], c_id[0:NE, 0:NE],
                   [B["logit8"], CONST], bPB[1], s4 == 0, s4 == 3,
                   tr=True)
            for s4 in range(4):
                col = s4 * NE
                vop((lambda col=col: lambda e: e.tensor_copy(
                    stackA[:, 0:NE], pb[1][:, col:col + NE]))(),
                    reads=[bPB[1]], writes=[B["cwsig"]])
                vop(lambda e: e.reduce_max(stackA[:, 16:17],
                                           stackA[:, 0:NE], axis=AX.X),
                    reads=[B["cwsig"]], writes=[B["cwm"]])
                vop(lambda e: e.tensor_scalar(
                    stackA[:, 24:24 + NE], stackA[:, 0:NE],
                    stackA[:, 16:17], None, ALU.is_ge),
                    reads=[B["cwsig"], B["cwm"]], writes=[B["cwmask"]])
                vop(lambda e: e.scalar_tensor_tensor(
                    stackA[:, 24:24 + NE], stackA[:, 24:24 + NE],
                    NEGBIG, stackA[:, 0:NE], ALU.mult, ALU.add),
                    reads=[B["cwsig"]], writes=[B["cwmask"]])
                vop(lambda e: e.reduce_max(stackA[:, 16:17],
                                           stackA[:, 24:24 + NE],
                                           axis=AX.X),
                    reads=[B["cwmask"]], writes=[B["cwm"]])
                sop(lambda e: e.activation(stackA[:, 32:32 + NE],
                                           stackA[:, 0:NE], AF.Sigmoid,
                                           bias=0.0, scale=1.0),
                    reads=[B["cwsig"]], writes=[B["cwsig"]])
                vop(lambda e: e.tensor_scalar(
                    stackA[:, 24:24 + NE], stackA[:, 0:NE],
                    stackA[:, 16:17], None, ALU.is_ge),
                    reads=[B["cwsig"], B["cwm"]], writes=[B["cwmask"]])
                vop(lambda e: e.tensor_tensor(
                    stackA[:, 40:40 + NE], stackA[:, 24:24 + NE],
                    stackA[:, 32:32 + NE], ALU.mult),
                    reads=[B["cwmask"], B["cwsig"]], writes=[B["cwsig"]])
                mm(pb[2][0:NE, s4 * C:(s4 + 1) * C],
                   stackA[:, 40:40 + NE], c_id[:], [B["cwsig"], CONST],
                   bPB[2], s4 == 0, s4 == 3, tr=True)
            vop(lambda e: e.tensor_copy(sml[0:NE, 3072:3072 + ST],
                                        pb[2][0:NE, 0:ST]),
                reads=[bPB[2]], writes=[buf("cwb")])
            em.dma("G", "qcw", (lambda t0=t0: lambda e: e.dma_start(
                out=cwst[:, t0:t0 + ST],
                in_=sml[0:NE, 3072:3072 + ST]))(),
                reads=[B["cwb"]], writes=[B["cwst"]])

        for e_i in range(-1, NE):
            for tt in range(NTT):
                t0 = tt * ST
                if e_i >= 0:
                    em.dma("G", "qcw", (lambda e_i=e_i, t0=t0:
                                        lambda e: e.dma_start(
                        out=stp[0:1, 0:ST],
                        in_=cwst[e_i:e_i + 1, t0:t0 + ST]))(),
                        reads=[B["cwst"]],
                        writes=[B["cwrow"], B["statr"]])
                    mm(pb[3][:, 0:ST], c_onesrow[0:1, :], stp[0:1, 0:ST],
                       [B["cwrow"]], bPB[3], True, True)
                for half in range(2):
                    for oc8 in range(8):
                        oc = half * 8 + oc8
                        if e_i < 0:
                            dg = dr["wgs"][oc]; du = dr["wus"][oc]
                        else:
                            dg = dr["wge"][e_i * NIC + oc]
                            du = dr["wue"][e_i * NIC + oc]
                        stg, sbg, qg, prg, prgb = next_upw()
                        ldsplit(qg, stg, sbg, prg, prgb, dg)
                        stu, sbu, qu, pru, prub = next_upw()
                        ldsplit(qu, stu, sbu, pru, prub, du)
                        for bank_i, pr, prb in ((0, prg, prgb),
                                                (1, pru, prub)):
                            ti = 0
                            for kc in range(NKC):
                                for wv, xv in ((pr[0], xh_ap(kc, t0)),
                                               (pr[0], xl_ap(kc, t0)),
                                               (pr[1], xh_ap(kc, t0))):
                                    mm(pb[bank_i][:, 0:ST],
                                       wv[:, kc * 128:(kc + 1) * 128],
                                       xv, [prb, bX2LN[kc]],
                                       bPB[bank_i], ti == 0,
                                       ti == 3 * NKC - 1)
                                    ti += 1
                        sop(lambda e: e.activation(
                            A(0, 128, HS_o, ST), pb[0][:, 0:ST],
                            AF.Silu, bias=0.0, scale=1.0),
                            reads=[bPB[0]], writes=[B["hsil"]])
                        vop(lambda e: e.scalar_tensor_tensor(
                            hscr[:, 0:ST], pb[1][:, 0:ST], 1.0,
                            A(0, 128, HS_o, ST), ALU.mult, ALU.mult),
                            reads=[bPB[1], B["hsil"]],
                            writes=[B["hscr"]])
                        if e_i >= 0:
                            vop(lambda e: e.tensor_tensor(
                                hscr[:, 0:ST], hscr[:, 0:ST],
                                pb[3][:, 0:ST], ALU.mult),
                                reads=[bPB[3]], writes=[B["hscr"]])
                        vop((lambda oc8=oc8: lambda e: e.tensor_copy(
                            hh_ap(oc8), hscr[:, 0:ST]))(),
                            reads=[B["hscr"]], writes=[bH[oc8]])
                        vop((lambda oc8=oc8: lambda e: e.tensor_tensor(
                            hl_ap(oc8), hscr[:, 0:ST], hh_ap(oc8),
                            ALU.subtract))(),
                            reads=[B["hscr"]], writes=[bH[oc8]])
                    for hcb in range(NKC):
                        if e_i < 0:
                            dd = dr["wds"][hcb]
                        else:
                            dd = dr["wde"][e_i * NKC + hcb]
                        std_, sbd, qd, prd, prdb = next_dnw()
                        ldsplit(qd, std_, sbd, prd, prdb,
                                dd[:, half * 1024:(half + 1) * 1024])
                        bank = bPB[4 + (hcb % 2)]
                        pbank = pb[4 + (hcb % 2)]
                        ti = 0
                        for oc8 in range(8):
                            for wv, hv in ((prd[0], hh_ap(oc8)),
                                           (prd[0], hl_ap(oc8)),
                                           (prd[1], hh_ap(oc8))):
                                mm(pbank[:, 0:ST],
                                   wv[:, oc8 * 128:(oc8 + 1) * 128],
                                   hv, [prdb, bH[oc8]], bank,
                                   ti == 0, ti == 23)
                                ti += 1
                        vop((lambda hcb=hcb, t0=t0, pbank=pbank:
                             lambda e: e.tensor_tensor(
                            xres_ap(hcb, t0, t0 + ST),
                            xres_ap(hcb, t0, t0 + ST),
                            pbank[:, 0:ST], ALU.add))(),
                            reads=[bank], writes=[bXR[hcb]])

        st_toks = []
        for kc in range(NKC):
            st_toks.append(em.dma("Y", "qout",
                                  (lambda kc=kc: lambda e: e.dma_start(
                out=yout[kc * 128:(kc + 1) * 128, :],
                in_=xres_ap(kc, 0, TM)))(), reads=[bXR[kc]]))
        em.op("Y", lambda e: e.nop(), extra=st_toks)

        return _emit(nc, em, ctx)


_BUILD_CACHE = {}


def _prep_static(inputs):
    w = {}
    w["wq"] = _tile_lhsT(inputs["L0_attn_Wq"])
    w["wk"] = _tile_lhsT(inputs["L0_attn_Wk"])
    w["wv"] = _tile_lhsT(inputs["L0_attn_Wv"])
    w["wgt"] = _tile_lhsT(inputs["L0_attn_Wg"])
    w["wo"] = _tile_lhsT(inputs["L0_attn_Wo"])
    w["wgs"] = _tile_lhsT(inputs["L0_ffn_Wg_s"])
    w["wus"] = _tile_lhsT(inputs["L0_ffn_Wu_s"])
    w["wds"] = _tile_lhsT(inputs["L0_ffn_Wd_s"])
    w["wge"] = np.concatenate([_tile_lhsT(inputs["L0_ffn_Wg"][e])
                               for e in range(NE)], 0)
    w["wue"] = np.concatenate([_tile_lhsT(inputs["L0_ffn_Wu"][e])
                               for e in range(NE)], 0)
    w["wde"] = np.concatenate([_tile_lhsT(inputs["L0_ffn_Wd"][e])
                               for e in range(NE)], 0)
    w["wbt"] = _tile_small(inputs["L0_attn_Wb"])
    w["wat"] = _tile_small(inputs["L0_attn_Wa"])
    w["wrt"] = _tile_small(inputs["L0_ffn_Wr"])
    for nm, src in (("convq", "L0_attn_conv_q"),
                    ("convk", "L0_attn_conv_k"),
                    ("convv", "L0_attn_conv_v")):
        cv = inputs[src]
        w[nm] = np.ascontiguousarray(
            cv.reshape(NKC, 128, 4).transpose(1, 0, 2).reshape(128,
                                                               NKC * 4))
    w["alog"] = np.ascontiguousarray(inputs["L0_attn_A_log"].reshape(NH, 1))
    w["dtb"] = np.ascontiguousarray(
        inputs["L0_attn_dt_bias"].reshape(NH, 1))
    w["normw2"] = np.ascontiguousarray(
        np.tile(inputs["L0_attn_norm_w"], 2).reshape(128, 1))
    w["rb8"] = np.ascontiguousarray(inputs["L0_ffn_rb"].reshape(NE, 1))
    w["ident"] = np.eye(128, dtype=np.float32)
    import numpy as _np
    lo, up, upi = _masks()
    w["mb_lo"] = _np.ascontiguousarray(_np.tile(lo, (1, 2)))
    w["mb_upi"] = _np.ascontiguousarray(_np.tile(upi, (1, 2)))
    w["hmask"] = _headmask()
    return w


def kernel(**inputs):
    inputs = {k: np.asarray(v, dtype=np.float32) for k, v in inputs.items()}
    x = inputs["L0_attn_hidden"]
    Bn, T, _ = x.shape
    halfT = T // 2
    w = _prep_static(inputs)
    if "full" not in _BUILD_CACHE:
        _BUILD_CACHE["full"] = build_nc(4, 4)
    nc = _BUILD_CACHE["full"]
    in_maps = []
    for core in range(8):
        b, hf = core // 2, core % 2
        m = dict(w)
        m["xmain"] = np.ascontiguousarray(
            x[b, hf * halfT:(hf + 1) * halfT].T)
        m["xpre"] = (np.zeros((H, halfT), np.float32) if hf == 0
                     else np.ascontiguousarray(x[b, 0:halfT].T))
        in_maps.append(m)
    res = run_bass_kernel_spmd(nc, in_maps, core_ids=list(range(8)))
    out = np.empty_like(x)
    for core in range(8):
        b, hf = core // 2, core % 2
        out[b, hf * halfT:(hf + 1) * halfT] = res.results[core]["y"].T
    return out


# version 09 (final)
# speedup vs baseline: 1.5561x; 1.5561x over previous
import contextlib
import numpy as np
import concourse.bass as bass
import concourse.mybir as mybir
from concourse.bass_utils import run_bass_kernel_spmd

F32 = mybir.dt.float32
AF = mybir.ActivationFunctionType
ALU = mybir.AluOpType
AX = mybir.AxisListType

H = 1024
NKC = H // 128
NH = 16
D = 64
C = 128
ST = 512
NCH = ST // C
I_FFN = 2048
NIC = I_FFN // 128
NE = 8
NEGBIG = -1.0e30


class Buf:
    __slots__ = ("w", "rs", "name")

    def __init__(self, name=""):
        self.w = None
        self.rs = []
        self.name = name


class Op:
    __slots__ = ("i", "eng", "fn", "deps", "queue", "need_inc", "val",
                 "waits", "pre_drain", "nodrain")

    def __init__(self, i, eng, fn, deps, queue):
        self.i = i
        self.eng = eng
        self.fn = fn
        self.deps = deps
        self.queue = queue
        self.need_inc = False
        self.val = None
        self.waits = []
        self.pre_drain = False
        self.nodrain = False


class Em:
    ENGS = ("T", "V", "S", "G", "Y")

    def __init__(self):
        self.all_ops = []
        self.queues = {}

    def op(self, eng, fn, reads=(), writes=(), extra=(), queue=None):
        deps = set()
        for b in reads:
            if b.w is not None:
                deps.add(b.w)
        for b in writes:
            if b.w is not None:
                deps.add(b.w)
            deps.update(b.rs)
        for t in extra:
            if t is not None:
                deps.add(t)
        o = Op(len(self.all_ops), eng, fn, sorted(deps), queue)
        self.all_ops.append(o)
        if queue is not None:
            self.queues.setdefault(queue, []).append(o.i)
        for b in writes:
            b.w = o.i
            b.rs = []
        for b in reads:
            b.rs.append(o.i)
        return o.i

    def dma(self, eng, queue, fn, reads=(), writes=(), extra=()):
        return self.op(eng, fn, reads=reads, writes=writes, extra=extra,
                       queue=queue)

    def finalize(self):
        ops = self.all_ops
        for o in ops:
            for d in o.deps:
                p = ops[d]
                if p.queue is None and p.eng == o.eng:
                    continue
                p.need_inc = True
        for q, ids in self.queues.items():
            for n, i in enumerate(ids):
                ops[i].need_inc = True
                ops[i].val = 16 * (n + 1)
        cnt = {e: 0 for e in self.ENGS}
        for o in ops:
            if o.queue is None and o.need_inc:
                cnt[o.eng] += 1
                o.val = cnt[o.eng]
        wm = {e: {} for e in self.ENGS}
        drain_cover = {e: -1 for e in self.ENGS}
        prev_op = {e: -1 for e in self.ENGS}
        for o in ops:
            need_drain = False
            cand = {}
            for d in o.deps:
                p = ops[d]
                if p.queue is None and p.eng == o.eng:
                    if d > drain_cover[o.eng]:
                        need_drain = True
                    continue
                key = ("q", p.queue) if p.queue is not None else ("e", p.eng)
                if p.val > cand.get(key, 0):
                    cand[key] = p.val
            for key, val in cand.items():
                if val <= wm[o.eng].get(key, 0):
                    continue
                wm[o.eng][key] = val
                o.waits.append((key, val))
            if need_drain:
                o.pre_drain = True
                drain_cover[o.eng] = prev_op[o.eng]
            if o.queue is None:
                prev_op[o.eng] = o.i
        return ops


def _tile_lhsT(w):
    O, Hin = w.shape
    wt = np.ascontiguousarray(w.T)
    a = wt.reshape(Hin // 128, 128, O // 128, 128)
    return np.ascontiguousarray(
        a.transpose(2, 1, 0, 3).reshape(O // 128, 128, Hin))


def _tile_small(w):
    O, Hin = w.shape
    wt = np.ascontiguousarray(w.T).reshape(Hin // 128, 128, O)
    return np.ascontiguousarray(
        wt.transpose(1, 0, 2).reshape(128, (Hin // 128) * O))


def _masks():
    r = np.arange(C)
    mb = lambda m: np.where(m, 0.0, NEGBIG).astype(np.float32)
    return (mb(r[:, None] > r[None, :]), mb(r[:, None] < r[None, :]),
            mb(r[:, None] <= r[None, :]))


def _headmask():
    hm = np.zeros((NKC, 128, NH), np.float32)
    for hc in range(NKC):
        hm[hc, 0:64, 2 * hc] = 1.0
        hm[hc, 64:128, 2 * hc + 1] = 1.0
    return np.ascontiguousarray(hm.transpose(1, 0, 2).reshape(128, NKC * NH))


def _emit(nc, em, ctx):
    import contextlib
    ops = em.finalize()
    sem_keys = set()
    for o in ops:
        for (key, _v) in o.waits:
            sem_keys.add(key)
        if o.need_inc:
            sem_keys.add(("q", o.queue) if o.queue is not None
                         else ("e", o.eng))
    semctx = contextlib.ExitStack()
    with semctx:
        sems = {}
        for i, key in enumerate(sorted(sem_keys, key=str)):
            sems[key] = semctx.enter_context(
                nc.semaphore(f"s{i}_{key[1]}"))
        per_eng = {e: [o for o in ops if o.eng == e] for e in Em.ENGS}

        def runner(eng_name):
            def go(eng):
                for o in per_eng[eng_name]:
                    if o.pre_drain:
                        eng.drain()
                    for (key, val) in o.waits:
                        eng.wait_ge(sems[key], val)
                    inst = o.fn(eng)
                    if o.need_inc:
                        key = (("q", o.queue) if o.queue is not None
                               else ("e", o.eng))
                        inst.then_inc(sems[key],
                                      16 if o.queue is not None else 1)
            return go

        with nc.Block() as block:
            block.tensor(runner("T"))
            block.vector(runner("V"))
            block.scalar(runner("S"))
            block.gpsimd(runner("G"))
            block.sync(runner("Y"))
    return nc


def build_nc(n_pre=4, n_main=4, skip_ffn=False, lim_chunks=None, skip_scan=False, lim_heads=None, skip_wo=False, hstages=('gd','eb','sq','tok','solve','wk','o','state'), dbg=False):
    TP = n_pre * ST
    TM = n_main * ST
    NTT = n_main
    nc = bass.Bass("TRN2", target_bir_lowering=False, debug=False)

    dr = {}
    def din(name, shape):
        dr[name] = nc.dram_tensor(name, list(shape), F32,
                                  kind="ExternalInput").ap()

    din("xmain", (H, TM))
    din("xpre", (H, TP))
    for nm in ("wq", "wk", "wv", "wgt", "wo"):
        din(nm, (NKC, 128, H))
    din("wgs", (NIC, 128, H)); din("wus", (NIC, 128, H))
    din("wds", (NKC, 128, I_FFN))
    din("wge", (NE * NIC, 128, H)); din("wue", (NE * NIC, 128, H))
    din("wde", (NE * NKC, 128, I_FFN))
    din("wbt", (128, NKC * NH)); din("wat", (128, NKC * NH))
    din("wrt", (128, NKC * NE))
    din("convq", (128, NKC * 4)); din("convk", (128, NKC * 4))
    din("convv", (128, NKC * 4))
    din("alog", (NH, 1)); din("dtb", (NH, 1)); din("normw2", (128, 1))
    din("rb8", (NE, 1))
    din("ident", (128, 128))
    din("mb_lo", (128, 2 * C)); din("mb_upi", (128, 2 * C))
    din("hmask", (128, NKC * NH))
    yout = nc.dram_tensor("y", [H, TM], F32, kind="ExternalOutput").ap()

    em = Em()
    B = {}
    def buf(name):
        if name not in B:
            B[name] = Buf(name)
        return B[name]

    ctx = contextlib.ExitStack()
    sb = lambda name, shape: ctx.enter_context(
        nc.sbuf_tensor(name, list(shape), F32))
    ps = lambda name, shape: ctx.enter_context(
        nc.psum_tensor(name, list(shape), F32))

    with ctx:
        x_res = sb("x_res", (128, NKC * TM))
        AR_COLS = 21472
        arena = sb("arena", (128, AR_COLS))
        upslots = [sb("up0", (128, H)), sb("up1", (128, H))]
        dnslots = [sb("dn0", (128, H)), sb("dn1", (128, H))]
        c_wbt = sb("c_wbt", (128, NKC * NH))
        c_wat = sb("c_wat", (128, NKC * NH))
        c_wrt = sb("c_wrt", (128, NKC * NE))
        c_cv = {"q": sb("c_convq", (128, NKC * 4)),
                "k": sb("c_convk", (128, NKC * 4)),
                "v": sb("c_convv", (128, NKC * 4))}
        c_alog = sb("c_alog", (NH, 1)); c_dtb = sb("c_dtb", (NH, 1))
        c_negA = sb("c_negA", (NH, 1)); c_normw2 = sb("c_normw2", (128, 1))
        c_rb8 = sb("c_rb8", (NE, 1))
        c_id = sb("c_id", (128, 128))
        c_mblo = sb("c_mblo", (128, 2 * C))
        c_mbupi = sb("c_mbupi", (128, 2 * C))
        c_hmask = sb("c_hmask", (128, NKC * NH))
        c_ones = sb("c_ones", (128, 1)); c_onesrow = sb("c_onesrow", (33, 128))
        c_eps5a = sb("c_eps5a", (1, 1))
        c_eps6 = sb("c_eps6", (NH, 1))
        c_eps5b = sb("c_eps5b", (128, 1))
        hist = {"q": sb("histq", (128, NKC * 4)),
                "k": sb("histk", (128, NKC * 4)),
                "v": sb("histv", (128, NKC * 4))}
        sml = sb("sml", (NH, 4228))
        g16 = sml[:, 0:ST]; beta16 = sml[:, 512:512 + ST]
        lnb16 = sml[:, 1024:1024 + ST]; lk16 = sml[:, 1536:1536 + ST]
        G16 = sml[:, 2048:2048 + ST]
        RA16 = sml[:, 2560:2560 + ST]
        RB16 = sml[:, 3072:3072 + ST]
        rq16 = sml[:, 3584:3584 + ST]
        t16a = RA16; t16b = RB16
        e116 = sml[:, 4096:4096 + C]
        lamc16 = sml[:, 4224:4225]
        logit8 = sml[0:NE, 2560:2560 + ST]
        stp = sb("stp", (33, ST))
        stat_r = stp[0:1, 0:ST]; stat_m = stp[32:33, 0:ST]
        aux = sb("aux", (128, 384))
        stackA = sb("stackA", (128, C)); stackB = sb("stackB", (NH, C))
        tokcol = sb("tokcol", (128, 144))
        lamrep = sb("lamrep", (128, NH))
        lam1 = lamrep[0:1, :]
        rfp = sb("rfp", (34, 2 * NH * C))
        cwst = rfp[0:NE, 0:TM]
        pb = [ps(f"pb{i}", (128, 512)) for i in range(8)]

        QO, KO, VO = 0, 4128, 8256
        XLN = 12384
        OFMC = 16480
        SCW = 17504
        PSL = [SCW, SCW + 512]
        E_o = SCW + 1024
        Bm_o = SCW + 1280
        BTm_o = SCW + 1536
        MTm_o = SCW + 1792
        UB_o = SCW + 2048
        U_o = SCW + 2304
        WKT_o = SCW + 2432
        KHAT_o = SCW + 2688
        OT_o = UB_o
        SQD_o = UB_o + 128
        MS_o = WKT_o
        S_ALL_o = SCW + 2816 + 128
        X2LN = 0
        HB = 16384
        HS_o = 20480
        SQ5_o = SCW

        def A(p0, p1, off, w):
            return arena[p0:p1, off:off + w]

        def qkv_ap(base, kc, a, b):
            return arena[:, base + kc * 516 + a: base + kc * 516 + b]

        def xln_ap(kc, a, b):
            return arena[:, XLN + kc * ST + a: XLN + kc * ST + b]

        def ofmc_ap(p0, p1, hc):
            return arena[p0:p1, OFMC + hc * C: OFMC + (hc + 1) * C]

        def x2ln_ap(kc, a, b):
            return arena[:, X2LN + kc * TM + a: X2LN + kc * TM + b]

        def h_ap(oc8, a, b):
            return arena[:, HB + oc8 * ST + a: HB + oc8 * ST + b]

        def xres_ap(kc, a, b):
            return x_res[:, kc * TM + a: kc * TM + b]

        def S_ap(p0, h):
            return arena[p0:p0 + 64, S_ALL_o + h * D: S_ALL_o + (h + 1) * D]

        bQ = [buf(f"q{k}") for k in range(NKC)]
        bK = [buf(f"k{k}") for k in range(NKC)]
        bV = [buf(f"v{k}") for k in range(NKC)]
        bXLN = [buf(f"xln{k}") for k in range(NKC)]
        bOFM = [buf(f"ofm{k}") for k in range(NKC)]
        bX2LN = [buf(f"x2ln{k}") for k in range(NKC)]
        bH = [buf(f"h{o}") for o in range(8)]
        bXR = [buf(f"xr{k}") for k in range(NKC)]
        bPB = [buf(f"pb{i}") for i in range(8)]
        bUP = [buf(f"up{i}") for i in range(2)]
        bDN = [buf(f"dn{i}") for i in range(2)]
        bPk = [buf(f"Pk{k}") for k in range(6)]
        bPTk = [buf(f"PTk{k}") for k in range(6)]
        for nm in ("consts g16 beta16 lnb16 lk16 rq16 G16 RA16 RB16 t16a "
                   "t16b e116 lamc lam1 stackA stackB tokcol lamrep S "
                   "sq512 cacc0 cacc1 statm statr statt cwrow eB eBT eM Bm "
                   "BTm MTm ub Ub wkT khat Otok sqd msr logit8 cwm cwmask "
                   "cwsig hsil cwst rfp hist xstage e2b lamb ps0 ps1").split():
            buf(nm)
        CONST = B["consts"]

        for nm, t in (("wbt", c_wbt), ("wat", c_wat), ("wrt", c_wrt),
                      ("convq", c_cv["q"]), ("convk", c_cv["k"]),
                      ("convv", c_cv["v"]), ("alog", c_alog),
                      ("dtb", c_dtb), ("normw2", c_normw2), ("rb8", c_rb8),
                      ("ident", c_id), ("mb_lo", c_mblo),
                      ("mb_upi", c_mbupi), ("hmask", c_hmask)):
            em.dma("Y", "qc", (lambda t=t, nm=nm: lambda e: e.dma_start(
                out=t[:], in_=dr[nm]))(), writes=[CONST])
        for kc in range(NKC):
            em.dma("Y", "qx", (lambda kc=kc: lambda e: e.dma_start(
                out=xres_ap(kc, 0, TM),
                in_=dr["xmain"][kc * 128:(kc + 1) * 128, :]))(),
                writes=[bXR[kc], B["xstage"]])

        def vset(ap, val, writes, reads=()):
            em.op("V", (lambda a=ap, v=val: lambda e: e.memset(a, v))(),
                  reads=list(reads), writes=list(writes))

        vset(c_ones[:], 1.0, [CONST])
        vset(c_onesrow[:], 1.0, [CONST])
        vset(c_eps5a[:], 1e-5, [CONST])
        vset(c_eps6[:], 1e-6, [CONST])
        vset(c_eps5b[:], 1e-5, [CONST])
        vset(arena[:, S_ALL_o:S_ALL_o + NH * D], 0.0, [B["S"]])
        vset(rfp[0:2, :], 1.0, [B["rfp"]])
        vset(rfp[32:34, :], 1.0, [B["rfp"]])
        vset(stackA[:], 0.0, [B["stackA"]])
        vset(stackB[:], 0.0, [B["stackB"]])
        for t in ("q", "k", "v"):
            vset(hist[t][:], 0.0, [B["hist"]])
        em.op("S", lambda e: e.activation(c_negA[:], c_alog[:], AF.Exp,
                                          bias=0.0, scale=1.0),
              reads=[CONST], writes=[CONST])
        em.op("V", lambda e: e.tensor_scalar_mul(c_negA[:], c_negA[:], -1.0),
              reads=[CONST], writes=[CONST])

        upq = [0]; dnq = [0]

        def next_up():
            i = upq[0] % 2; upq[0] += 1
            return upslots[i], bUP[i], f"qu{i}"

        def next_dn():
            i = dnq[0] % 2; dnq[0] += 1
            return dnslots[i], bDN[i], f"qd{i}"

        def ld(queue, slot_t, slot_b, dram_ap, width):
            em.dma("Y", queue, (lambda t=slot_t, d=dram_ap, w=width:
                                lambda e: e.dma_start(out=t[:, 0:w],
                                                      in_=d))(),
                   writes=[slot_b])

        def mm(out_ap, lhsT_ap, rhs_ap, reads, wbank, start, stop,
               tr=False):
            i = em.op("T", (lambda o=out_ap, l=lhsT_ap, r=rhs_ap,
                            s=start, p=stop, t=tr: lambda e: e.matmul(
                o, l, r, start=s, stop=p,
                is_transpose=(True if t else None)))(),
                reads=list(reads), writes=[wbank])
            if not start:
                em.all_ops[i].nodrain = True
            return i

        def vop(fn, reads=(), writes=()):
            em.op("V", fn, reads=list(reads), writes=list(writes))

        def sop(fn, reads=(), writes=()):
            em.op("S", fn, reads=list(reads), writes=list(writes))

        def ln_pass(src_fn, src_bufs, dst_fn, dst_bufs):
            for kc in range(NKC):
                sop((lambda kc=kc: lambda e: e.activation(
                    A(0, 128, SQ5_o, ST), src_fn(kc), AF.Square, bias=0.0,
                    scale=1.0))(), reads=[src_bufs[kc]],
                    writes=[B["sq512"], B["ps0"], B["ps1"]])
                mm(pb[7][0:1, 0:ST], c_ones[:], src_fn(kc),
                   [CONST, src_bufs[kc]], bPB[7], kc == 0, kc == NKC - 1)
                mm(pb[6][0:1, 0:ST], c_ones[:], A(0, 128, SQ5_o, ST),
                   [CONST, B["sq512"]], bPB[6], kc == 0, kc == NKC - 1)
            vop(lambda e: e.tensor_scalar_mul(stat_m, pb[7][0:1, 0:ST],
                                              1.0 / H),
                reads=[bPB[7]], writes=[B["statm"]])
            vop(lambda e: e.scalar_tensor_tensor(
                stat_r, stat_m, -1.0, stat_m, ALU.mult, ALU.mult),
                reads=[B["statm"]], writes=[B["statr"]])
            vop(lambda e: e.scalar_tensor_tensor(
                stat_r, pb[6][0:1, 0:ST], 1.0 / H, stat_r, ALU.mult,
                ALU.add),
                reads=[bPB[6]], writes=[B["statr"]])
            sop(lambda e: e.activation(stat_r, stat_r, AF.Sqrt,
                                       bias=c_eps5a[:], scale=1.0),
                reads=[CONST], writes=[B["statr"]])
            vop(lambda e: e.reciprocal(stat_r, stat_r),
                reads=[], writes=[B["statr"]])
            mm(pb[7][:, 0:ST], c_onesrow[32:33, :], stat_m,
               [CONST, B["statm"]], bPB[7], True, True)
            mm(pb[6][:, 0:ST], c_onesrow[0:1, :], stat_r,
               [CONST, B["statr"]], bPB[6], True, True)
            for kc in range(NKC):
                vop((lambda kc=kc: lambda e: e.tensor_tensor(
                    dst_fn(kc), src_fn(kc), pb[7][:, 0:ST],
                    ALU.subtract))(),
                    reads=[src_bufs[kc], bPB[7]], writes=[dst_bufs[kc]])
                vop((lambda kc=kc: lambda e: e.tensor_tensor(
                    dst_fn(kc), dst_fn(kc), pb[6][:, 0:ST], ALU.mult))(),
                    reads=[bPB[6]], writes=[dst_bufs[kc]])

        def proj16(wt_const, width, bank_i, nrows):
            for kc in range(NKC):
                mm(pb[bank_i][0:nrows, 0:ST],
                   wt_const[:, kc * width:(kc + 1) * width],
                   xln_ap(kc, 0, ST), [CONST, bXLN[kc]], bPB[bank_i],
                   kc == 0, kc == NKC - 1)

        def softplus_into(dst_ap, dst_b, src_ap, src_b, negate_arg):
            if negate_arg:
                vop((lambda s=src_ap: lambda e: e.tensor_scalar_mul(
                    t16a, s, -1.0))(),
                    reads=[src_b], writes=[B["RA16"]])
            else:
                vop((lambda s=src_ap: lambda e: e.tensor_scalar(
                    t16a, s, c_dtb[:], None, ALU.add))(),
                    reads=[src_b, CONST], writes=[B["RA16"]])
            sop(lambda e: e.activation(t16b, t16a, AF.Abs, bias=0.0,
                                       scale=1.0),
                reads=[B["RA16"]], writes=[B["RB16"]])
            sop(lambda e: e.activation(t16b, t16b, AF.Exp, bias=0.0,
                                       scale=-1.0),
                reads=[], writes=[B["RB16"]])
            sop(lambda e: e.activation(t16b, t16b, AF.Ln, bias=1.0,
                                       scale=1.0),
                reads=[], writes=[B["RB16"]])
            vop(lambda e: e.tensor_scalar_max(t16a, t16a, 0.0),
                reads=[], writes=[B["RA16"]])
            vop((lambda d=dst_ap: lambda e: e.tensor_tensor(
                d, t16a, t16b, ALU.add))(),
                reads=[B["RA16"], B["RB16"]], writes=[dst_b])

        def hp_mk(cc, j, is_main):
            hc = j
            hA, hB = 2 * j, 2 * j + 1
            pi = j % 2
            kT = {0: A(0, 64, KO + hc * 516 + 4 + cc, C),
                  64: A(64, 128, KO + hc * 516 + 4 + cc, C)}
            vT = {0: A(0, 64, VO + hc * 516 + 4 + cc, C),
                  64: A(64, 128, VO + hc * 516 + 4 + cc, C)}
            qT = {0: A(0, 64, QO + hc * 516 + 4 + cc, C),
                  64: A(64, 128, QO + hc * 516 + 4 + cc, C)}
            bk, bv, bq = bK[hc], bV[hc], bQ[hc]
            bS = B["S"]
            eAB = A(0, 128, E_o, 2 * C)
            BmAB = A(0, 128, Bm_o, 2 * C)
            Bm = {0: A(0, 128, Bm_o, C), 64: A(0, 128, Bm_o + C, C)}
            BTmAB = A(0, 128, BTm_o, 2 * C)
            BTm = {0: A(0, 128, BTm_o, C), 64: A(0, 128, BTm_o + C, C)}
            if pi == 0:
                MTmAB = A(0, 128, MTm_o, 2 * C)
                MTm = {0: A(0, 128, MTm_o, C),
                       64: A(0, 128, MTm_o + C, C)}
                bMT = buf("MTm")
                khath = {0: A(0, 128, KHAT_o, D),
                         64: A(0, 128, KHAT_o + D, D)}
                bKH = buf("khat")
                ub = A(0, 128, UB_o, 2 * C)
                ubh = {0: A(0, 128, UB_o, C), 64: A(0, 128, UB_o + C, C)}
                Otokh = {0: A(0, 128, UB_o, D), 64: A(0, 128, UB_o + D, D)}
                sqd = lambda o: A(0, 128, UB_o + 128 + o, D)
                bUB = buf("ub")
            else:
                MTmAB = dnslots[0][:, 384:640]
                MTm = {0: dnslots[0][:, 384:512],
                       64: dnslots[0][:, 512:640]}
                bMT = buf("MTm2")
                khath = {0: dnslots[0][:, 256:320],
                         64: dnslots[0][:, 320:384]}
                bKH = buf("khat2")
                ub = dnslots[0][:, 0:256]
                ubh = {0: dnslots[0][:, 0:128], 64: dnslots[0][:, 128:256]}
                Otokh = {0: dnslots[0][:, 0:D], 64: dnslots[0][:, D:2 * D]}
                sqd = lambda o: dnslots[0][:, 128 + o:128 + o + D]
                bUB = buf("ub2")
            UAB = A(0, 128, U_o, 2 * D)
            Uh = {0: A(0, 128, U_o, D), 64: A(0, 128, U_o + D, D)}
            wkTh = {0: A(0, 64, WKT_o, C), 64: A(0, 64, WKT_o + C, C)}
            PS = [A(0, 128, PSL[0], 512), A(0, 128, PSL[1], 512)]
            bPS = [B["ps0"], B["ps1"]]
            ro = {0: hA * C, 64: hB * C}
            RO2 = NH * C
            bcol = lambda q, h: tokcol[:, 32 * q + h:32 * q + h + 1]

            def part_fq():
                mm(pb[0][:, 0:C], kT[0], kT[0], [bk], bPB[0], True, False)
                mm(pb[0][:, C:2 * C], kT[64], kT[64], [bk], bPB[0], False,
                   not is_main)
                if is_main:
                    mm(pb[0][:, 2 * C:3 * C], kT[0], qT[0], [bk, bq],
                       bPB[0], False, False)
                    mm(pb[0][:, 3 * C:4 * C], kT[64], qT[64], [bk, bq],
                       bPB[0], False, True)

            def part_gd():
                mm(pb[1][:, 0:C], rfp[0:2, ro[0]:ro[0] + C],
                   rfp[0:2, RO2 + ro[0]:RO2 + ro[0] + C], [B["rfp"]],
                   bPB[1], True, False)
                mm(pb[1][:, C:2 * C], rfp[0:2, ro[64]:ro[64] + C],
                   rfp[0:2, RO2 + ro[64]:RO2 + ro[64] + C], [B["rfp"]],
                   bPB[1], False, not is_main)
                if is_main:
                    mm(pb[1][:, 2 * C:3 * C], rfp[32:34, ro[0]:ro[0] + C],
                       rfp[32:34, RO2 + ro[0]:RO2 + ro[0] + C],
                       [B["rfp"]], bPB[1], False, False)
                    mm(pb[1][:, 3 * C:4 * C],
                       rfp[32:34, ro[64]:ro[64] + C],
                       rfp[32:34, RO2 + ro[64]:RO2 + ro[64] + C],
                       [B["rfp"]], bPB[1], False, True)

            def part_e():
                vop(lambda e: e.tensor_tensor(eAB, pb[1][:, 0:2 * C],
                                              c_mblo[:], ALU.add),
                    reads=[bPB[1], CONST], writes=[B["eB"]])
                sop(lambda e: e.activation(eAB, eAB, AF.Exp, bias=0.0,
                                           scale=1.0),
                    reads=[], writes=[B["eB"]])
                vop(lambda e: e.tensor_tensor(BmAB, pb[0][:, 0:2 * C],
                                              eAB, ALU.mult),
                    reads=[bPB[0], B["eB"]], writes=[B["Bm"]])
                if is_main:
                    vop(lambda e: e.tensor_tensor(
                        eAB, pb[1][:, 2 * C:4 * C], c_mbupi[:], ALU.add),
                        reads=[bPB[1], CONST], writes=[B["eB"]])
                    sop(lambda e: e.activation(eAB, eAB, AF.Exp, bias=0.0,
                                               scale=1.0),
                        reads=[], writes=[B["eB"]])
                    vop(lambda e: e.tensor_tensor(
                        MTmAB, pb[0][:, 2 * C:4 * C], eAB, ALU.mult),
                        reads=[bPB[0], B["eB"]], writes=[bMT])

            def part_t():
                mm(pb[1][:, 0:C], Bm[0], c_id[:], [B["Bm"], CONST],
                   bPB[1], True, False, tr=True)
                mm(pb[1][:, C:2 * C], Bm[64], c_id[:], [B["Bm"], CONST],
                   bPB[1], False, True, tr=True)
                vop(lambda e: e.tensor_copy(BTmAB, pb[1][:, 0:2 * C]),
                    reads=[bPB[1]], writes=[B["BTm"]])
                mm(pb[4][:, 0:D], vT[0], c_id[0:64, 0:64], [bv, CONST],
                   bPB[4], True, False, tr=True)
                mm(pb[4][:, D:2 * D], kT[0], c_id[0:64, 0:64], [bk, CONST],
                   bPB[4], False, False, tr=True)
                mm(pb[4][:, 2 * D:3 * D], vT[64], c_id[64:128, 64:128],
                   [bv, CONST], bPB[4], False, False, tr=True)
                mm(pb[4][:, 3 * D:4 * D], kT[64], c_id[64:128, 64:128],
                   [bk, CONST], bPB[4], False, True, tr=True)

            def part_u():
                for (p0, h, o4) in ((0, hA, 0), (64, hB, 2)):
                    vop((lambda p0=p0, h=h, o4=o4:
                         lambda e: e.tensor_scalar(
                        ub[:, o4 * D:(o4 + 1) * D],
                        pb[4][:, o4 * D:(o4 + 1) * D],
                        bcol(0, h), None, ALU.mult))(),
                        reads=[bPB[4], B["tokcol"]], writes=[bUB])
                    vop((lambda p0=p0, h=h, o4=o4:
                         lambda e: e.tensor_scalar(
                        ub[:, (o4 + 1) * D:(o4 + 2) * D],
                        pb[4][:, (o4 + 1) * D:(o4 + 2) * D],
                        bcol(2, h), None, ALU.mult))(),
                        reads=[bPB[4], B["tokcol"]], writes=[bUB])
                    vop((lambda p0=p0, h=h, o4=o4:
                         lambda e: e.tensor_scalar(
                        khath[p0], pb[4][:, (o4 + 1) * D:(o4 + 2) * D],
                        bcol(1, h), None, ALU.mult))(),
                        reads=[bPB[4], B["tokcol"]], writes=[bKH])

            def solve(weave=()):
                mm(pb[5][:, 0:C], BTm[0], ubh[0], [B["BTm"], bUB],
                   bPB[5], True, False)
                mm(pb[5][:, C:2 * C], BTm[64], ubh[64], [B["BTm"], bUB],
                   bPB[5], False, True)
                vop(lambda e: e.scalar_tensor_tensor(
                    ub, pb[5][:, 0:2 * C], -1.0, ub, ALU.mult, ALU.add),
                    reads=[bPB[5]], writes=[bUB])
                for L in range(1, 7):
                    sl_i = L % 2
                    bank = bPB[2 + sl_i]
                    pbank = pb[2 + sl_i]
                    if L == 1:
                        lA, rA = BTm[0], Bm[0]
                        lB, rB = BTm[64], Bm[64]
                        rdep = [B["Bm"], B["BTm"]]
                    else:
                        lA = A(0, 128, PSL[(L - 1) % 2] + C, C)
                        rA = A(0, 128, PSL[(L - 1) % 2], C)
                        lB = A(0, 128, PSL[(L - 1) % 2] + 3 * C, C)
                        rB = A(0, 128, PSL[(L - 1) % 2] + 2 * C, C)
                        rdep = [bPS[(L - 1) % 2]]
                    mm(pbank[:, 0:C], lA, rA, rdep, bank, True, False)
                    mm(pbank[:, C:2 * C], rA, lA, rdep, bank, False, False)
                    mm(pbank[:, 2 * C:3 * C], lB, rB, rdep, bank, False,
                       False)
                    mm(pbank[:, 3 * C:4 * C], rB, lB, rdep, bank, False,
                       True)
                    vop((lambda sl_i=sl_i, pbank=pbank:
                         lambda e: e.tensor_copy(PS[sl_i], pbank[:]))(),
                        reads=[bank],
                        writes=[bPS[sl_i]] + ([B["sq512"]] if sl_i == 0
                                              else []))
                    if L - 1 < len(weave):
                        weave[L - 1]()
                    mm(pb[5][:, 0:C], A(0, 128, PSL[sl_i] + C, C), ubh[0],
                       [bPS[sl_i], bUB], bPB[5], True, False)
                    mm(pb[5][:, C:2 * C], A(0, 128, PSL[sl_i] + 3 * C, C),
                       ubh[64], [bPS[sl_i], bUB], bPB[5], False, True)
                    vop(lambda e: e.tensor_tensor(
                        ub, ub, pb[5][:, 0:2 * C], ALU.add),
                        reads=[bPB[5]], writes=[bUB])

            def tail():
                mm(pb[6][0:64, 0:C], ub[:, D:2 * D], c_id[:],
                   [bUB, CONST], bPB[6], True, False, tr=True)
                mm(pb[6][0:64, C:2 * C], ub[:, 3 * D:4 * D], c_id[:],
                   [bUB, CONST], bPB[6], False, True, tr=True)
                vop(lambda e: e.tensor_copy(A(0, 64, WKT_o, 2 * C),
                                            pb[6][0:64, 0:2 * C]),
                    reads=[bPB[6]], writes=[B["wkT"]])
                mm(pb[6][:, 2 * C:2 * C + D], wkTh[0], S_ap(0, hA),
                   [B["wkT"], bS], bPB[6], True, False)
                mm(pb[6][:, 2 * C + D:2 * C + 2 * D], wkTh[64],
                   S_ap(0, hB), [B["wkT"], bS], bPB[6], False, True)
                vop(lambda e: e.scalar_tensor_tensor(
                    UAB, pb[6][:, 2 * C:2 * C + 2 * D], -1.0,
                    ub.rearrange("p (h w) -> p h w", h=2)[:, :, 0:D],
                    ALU.mult, ALU.add),
                    reads=[bPB[6], bUB], writes=[B["Ub"]])
                if is_main:
                    mm(pb[7][:, 0:D], MTm[0], Uh[0], [bMT, B["Ub"]],
                       bPB[7], True, False)
                    mm(pb[7][:, D:2 * D], qT[0], S_ap(0, hA), [bq, bS],
                       bPB[7], False, False)
                    mm(pb[7][:, 2 * D:3 * D], MTm[64], Uh[64],
                       [bMT, B["Ub"]], bPB[7], False, False)
                    mm(pb[7][:, 3 * D:4 * D], qT[64], S_ap(64, hB),
                       [bq, bS], bPB[7], False, True)
                    for (p0, h, o4) in ((0, hA, 0), (64, hB, 2)):
                        vop((lambda p0=p0, h=h, o4=o4:
                             lambda e: e.tensor_copy(
                            Otokh[p0], pb[7][:, o4 * D:(o4 + 1) * D]))(),
                            reads=[bPB[7]], writes=[bUB])
                        vop((lambda p0=p0, h=h, o4=o4:
                             lambda e: e.scalar_tensor_tensor(
                            Otokh[p0],
                            pb[7][:, (o4 + 1) * D:(o4 + 2) * D],
                            bcol(3, h), Otokh[p0], ALU.mult, ALU.add))(),
                            reads=[bPB[7], B["tokcol"]], writes=[bUB])
                        vop((lambda p0=p0, h=h: lambda e: e.tensor_scalar(
                            Otokh[p0], Otokh[p0],
                            tokcol[:, 128 + h:129 + h], None,
                            ALU.mult))(),
                            reads=[B["tokcol"]], writes=[bUB])
                    sop(lambda e: e.activation(
                        sqd(0), Otokh[0], AF.Square, bias=0.0,
                        scale=1.0, accum_out=A(0, 128, MS_o, 1)),
                        reads=[bUB], writes=[bUB, B["wkT"]])
                    sop(lambda e: e.activation(
                        sqd(D), Otokh[64], AF.Square,
                        bias=0.0, scale=1.0,
                        accum_out=A(0, 128, MS_o + 1, 1)),
                        reads=[bUB], writes=[bUB, B["wkT"]])
                    sop(lambda e: e.activation(
                        A(0, 128, MS_o + 2, 2), A(0, 128, MS_o, 2),
                        AF.Sqrt, bias=c_eps5b[:], scale=1.0 / D),
                        reads=[B["wkT"], CONST], writes=[B["wkT"]])
                    vop(lambda e: e.reciprocal(A(0, 128, MS_o + 4, 2),
                                               A(0, 128, MS_o + 2, 2)),
                        reads=[], writes=[B["wkT"]])
                    vop(lambda e: e.tensor_scalar(
                        Otokh[0], Otokh[0], A(0, 128, MS_o + 4, 1), None,
                        ALU.mult),
                        reads=[B["wkT"]], writes=[bUB])
                    vop(lambda e: e.tensor_scalar(
                        Otokh[64], Otokh[64], A(0, 128, MS_o + 5, 1),
                        None, ALU.mult),
                        reads=[B["wkT"]], writes=[bUB])
                    mm(pb[6][0:64, 0:C], Otokh[0], c_id[:],
                       [bUB, CONST], bPB[6], True, False, tr=True)
                    mm(pb[6][0:64, C:2 * C], Otokh[64], c_id[:],
                       [bUB, CONST], bPB[6], False, True, tr=True)
                    vop((lambda hc=hc: lambda e: e.tensor_scalar(
                        ofmc_ap(0, 64, hc), pb[6][0:64, 0:C],
                        c_normw2[0:64, :], None, ALU.mult))(),
                        reads=[bPB[6], CONST], writes=[bOFM[hc]])
                    vop((lambda hc=hc: lambda e: e.tensor_scalar(
                        ofmc_ap(64, 128, hc), pb[6][0:64, C:2 * C],
                        c_normw2[64:128, :], None, ALU.mult))(),
                        reads=[bPB[6], CONST], writes=[bOFM[hc]])
                mm(pb[4][0:64, 256:256 + D], khath[0], Uh[0],
                   [bKH, B["Ub"]], bPB[4], True, False)
                mm(pb[4][0:64, 256 + D:256 + 2 * D], khath[64], Uh[64],
                   [bKH, B["Ub"]], bPB[4], False, True)
                mm(pb[4][64:128, 256 + 2 * D:256 + 3 * D], khath[0],
                   Uh[0], [bKH, B["Ub"]], bPB[4], True, False)
                mm(pb[4][64:128, 256 + 3 * D:256 + 4 * D], khath[64],
                   Uh[64], [bKH, B["Ub"]], bPB[4], False, True)
                for (h, col, prow) in ((hA, 256, 0), (hB, 256 + D, 0),
                                       (hA, 256 + 2 * D, 64),
                                       (hB, 256 + 3 * D, 64)):
                    em.op("V", (lambda h=h, col=col, prow=prow:
                                lambda e: e.scalar_tensor_tensor(
                        S_ap(prow, h), S_ap(prow, h),
                        lamrep[prow:prow + 64, h:h + 1],
                        pb[4][prow:prow + 64, col:col + D], ALU.mult,
                        ALU.add))(),
                        reads=[bPB[4], B["lamrep"]], writes=[bS])

            return {"parts": [part_fq, part_gd, part_e, part_t, part_u],
                    "solve": solve, "tail": tail}

        def chunk(t0, cc, is_main):
            sl = slice(cc, cc + C)
            e2b = g16[:, sl]
            lamb = lnb16[:, sl]
            sop((lambda cc=cc: lambda e: e.activation(
                lamc16, G16[:, cc + C - 1:cc + C], AF.Exp, bias=0.0,
                scale=1.0))(), reads=[B["G16"]], writes=[B["lamc"]])
            vop((lambda sl=sl, cc=cc: lambda e: e.tensor_scalar(
                e116, RB16[:, sl], G16[:, cc + C - 1:cc + C], None,
                ALU.add))(), reads=[B["RB16"], B["G16"]],
                writes=[B["e116"]])
            sop(lambda e: e.activation(e116, e116, AF.Exp, bias=0.0,
                                       scale=1.0),
                reads=[], writes=[B["e116"]])
            sop((lambda sl=sl, e2b=e2b: lambda e: e.activation(
                e2b, RA16[:, sl], AF.Exp, bias=0.0, scale=1.0))(),
                reads=[B["RA16"]], writes=[B["g16"]])
            sop((lambda sl=sl, lamb=lamb: lambda e: e.activation(
                lamb, G16[:, sl], AF.Exp, bias=0.0, scale=1.0))(),
                reads=[B["G16"]], writes=[B["lnb16"]])
            def skdma(dst_p, src_ap, src_buf, dstt=None):
                dt = stackA if dstt is None else dstt
                em.dma("G", "qsk" if dstt is None else "qsk2",
                       (lambda dp=dst_p, s=src_ap, dt=dt:
                        lambda e: e.dma_start(
                    out=dt[dp:dp + NH, :], in_=s))(),
                    reads=[src_buf],
                    writes=[B["stackA"] if dstt is None else B["stackB"]])
            skdma(0, beta16[:, sl], B["beta16"])
            skdma(32, e116, B["e116"])
            skdma(64, e2b, B["g16"])
            skdma(96, lamb, B["lnb16"])
            if is_main:
                skdma(0, rq16[:, sl], B["rq16"], dstt=stackB)
            mm(pb[6][:, 0:128], stackA[:], c_id[:], [B["stackA"], CONST],
               bPB[6], True, not is_main, tr=True)
            if is_main:
                mm(pb[6][:, 128:128 + NH], stackB[0:NH, :],
                   c_id[0:NH, 0:NH], [B["stackB"], CONST], bPB[6],
                   False, True, tr=True)
            vop(lambda e: e.tensor_copy(tokcol[:, 0:128],
                                        pb[6][:, 0:128]),
                reads=[bPB[6]], writes=[B["tokcol"]])
            if is_main:
                vop(lambda e: e.tensor_copy(tokcol[:, 128:144],
                                            pb[6][:, 128:128 + NH]),
                    reads=[bPB[6]], writes=[B["tokcol"]])
            mm(pb[7][0:1, 0:NH], lamc16, c_id[0:NH, 0:NH],
               [B["lamc"], CONST], bPB[7], True, True, tr=True)
            vop(lambda e: e.tensor_copy(lam1, pb[7][0:1, 0:NH]),
                reads=[bPB[7]], writes=[B["lamrep"]])
            mm(pb[7][:, 64:64 + NH], c_onesrow[0:1, :], lam1,
               [CONST, B["lamrep"]], bPB[7], True, True)
            vop(lambda e: e.tensor_copy(lamrep[:], pb[7][:, 64:64 + NH]),
                reads=[bPB[7]], writes=[B["lamrep"]])

            def rfdma(dst_row, dst_off, src_ap, src_buf):
                em.dma("G", "qrf", (lambda r=dst_row, o=dst_off, s=src_ap:
                                    lambda e: e.dma_start(
                    out=rfp[r:r + 1, o:o + NH * C].rearrange(
                        "o (h t) -> o h t", h=NH),
                    in_=s))(), reads=[src_buf], writes=[B["rfp"]])
            RO2 = NH * C
            rfdma(0, 0, RA16[:, sl], B["RA16"])
            rfdma(1, RO2, RB16[:, sl], B["RB16"])
            rfdma(32, 0, RB16[:, sl], B["RB16"])
            rfdma(33, RO2, G16[:, sl], B["G16"])

            npair = 8 if lim_heads is None else max(1, lim_heads // 2)
            cur = hp_mk(cc, 0, is_main)
            for p in cur["parts"]:
                p()
            for j in range(npair):
                nxt = (hp_mk(cc, j + 1, is_main) if j + 1 < npair
                       else None)
                cur["solve"](nxt["parts"] if nxt is not None else ())
                cur["tail"]()
                cur = nxt

            if is_main and skip_wo:
                return
            if is_main:
                for hc in range(NKC):
                    vop((lambda hc=hc, cc=cc: lambda e: e.tensor_tensor(
                        ofmc_ap(0, 128, hc), ofmc_ap(0, 128, hc),
                        xln_ap(hc, cc, cc + C), ALU.mult))(),
                        reads=[bXLN[hc]], writes=[bOFM[hc]])
                for oc in range(NKC):
                    slot, sb_, qn = next_up()
                    ld(qn, slot, sb_, dr["wo"][oc], H)
                    bank = bPB[4 + (oc % 2)]
                    pbank = pb[4 + (oc % 2)]
                    qtr = (oc // 2) % 2
                    for kc in range(NKC):
                        mm(pbank[:, qtr * C:(qtr + 1) * C],
                           slot[:, kc * 128:(kc + 1) * 128],
                           ofmc_ap(0, 128, kc), [sb_, bOFM[kc]], bank,
                           kc == 0, kc == NKC - 1)
                    vop((lambda oc=oc, t0=t0, cc=cc, pbank=pbank,
                         qtr=qtr: lambda e: e.tensor_tensor(
                        xres_ap(oc, t0 + cc, t0 + cc + C),
                        xres_ap(oc, t0 + cc, t0 + cc + C),
                        pbank[:, qtr * C:(qtr + 1) * C], ALU.add))(),
                        reads=[bank], writes=[bXR[oc]])

        def supertile(sti, is_main, qproj):
            t0 = sti * ST
            if is_main:
                src_fn = lambda kc: xres_ap(kc, t0, t0 + ST)
                src_bufs = bXR
            else:
                for kc in range(NKC):
                    em.dma("Y", "qx", (lambda kc=kc, t0=t0:
                                       lambda e: e.dma_start(
                        out=xln_ap(kc, 0, ST),
                        in_=dr["xpre"][kc * 128:(kc + 1) * 128,
                                       t0:t0 + ST]))(),
                        writes=[bXLN[kc], B["xstage"]])
                src_fn = lambda kc: xln_ap(kc, 0, ST)
                src_bufs = bXLN
            ln_pass(src_fn, src_bufs, lambda kc: xln_ap(kc, 0, ST), bXLN)
            proj16(c_wbt, NH, 0, NH)
            sop(lambda e: e.activation(beta16, pb[0][0:NH, 0:ST],
                                       AF.Sigmoid, bias=0.0, scale=1.0),
                reads=[bPB[0]], writes=[B["beta16"]])
            softplus_into(lnb16, B["lnb16"], pb[0][0:NH, 0:ST], bPB[0],
                          negate_arg=True)
            vop(lambda e: e.tensor_scalar_mul(lnb16, lnb16, -1.0),
                reads=[], writes=[B["lnb16"]])
            proj16(c_wat, NH, 1, NH)
            softplus_into(g16, B["g16"], pb[1][0:NH, 0:ST], bPB[1],
                          negate_arg=False)
            vop(lambda e: e.tensor_scalar(g16, g16, c_negA[:], None,
                                          ALU.mult),
                reads=[CONST], writes=[B["g16"]])
            for c in range(NCH):
                vop((lambda c=c: lambda e: e.tensor_tensor_scan(
                    G16[:, c * C:(c + 1) * C], g16[:, c * C:(c + 1) * C],
                    g16[:, c * C:(c + 1) * C], 0.0, ALU.add,
                    ALU.bypass))(),
                    reads=[B["g16"]], writes=[B["G16"]])
            tensors = [("k", "wk", KO, bK), ("v", "wv", VO, bV)]
            if qproj:
                tensors.append(("q", "wq", QO, bQ))
            for (tn, wnm, base, bufs) in tensors:
                vop((lambda tn=tn, base=base: lambda e: e.tensor_copy(
                    arena[:, base:base + NKC * 516].rearrange(
                        "p (kc c) -> p kc c", c=516)[:, :, 0:4],
                    hist[tn][:].rearrange("p (kc c) -> p kc c", c=4)))(),
                    reads=[B["hist"]],
                    writes=[bufs[k] for k in range(NKC)])
                for oc in range(NKC):
                    slot, sb_, qn = next_up()
                    ld(qn, slot, sb_, dr[wnm][oc], H)
                    bank = bPB[2 + (oc % 2)]
                    pbank = pb[2 + (oc % 2)]
                    for kc in range(NKC):
                        mm(pbank[:, 0:ST],
                           slot[:, kc * 128:(kc + 1) * 128],
                           xln_ap(kc, 0, ST), [sb_, bXLN[kc]], bank,
                           kc == 0, kc == NKC - 1)
                    vop((lambda oc=oc, base=base, pbank=pbank:
                         lambda e: e.tensor_copy(
                        qkv_ap(base, oc, 4, 516), pbank[:, 0:ST]))(),
                        reads=[bank], writes=[bufs[oc]])
                    vop((lambda oc=oc, base=base, tn=tn:
                         lambda e: e.tensor_copy(
                        hist[tn][:, oc * 4:(oc + 1) * 4],
                        qkv_ap(base, oc, 512, 516)))(),
                        reads=[bufs[oc]], writes=[B["hist"]])
                for oc in range(NKC):
                    acc = pb[6 + (oc % 2)][:, 0:ST]
                    acb = bPB[6 + (oc % 2)]
                    cw = c_cv[tn]
                    for j in range(4):
                        eng = "V"
                        if j == 0:
                            em.op(eng, (lambda oc=oc, base=base, cw=cw,
                                        acc=acc:
                                        lambda e: e.tensor_scalar(
                                acc, qkv_ap(base, oc, 1, 513),
                                cw[:, oc * 4:oc * 4 + 1], None,
                                ALU.mult))(),
                                reads=[bufs[oc], CONST], writes=[acb])
                        else:
                            em.op(eng, (lambda oc=oc, base=base, cw=cw,
                                        j=j, acc=acc:
                                        lambda e:
                                        e.scalar_tensor_tensor(
                                acc, qkv_ap(base, oc, 1 + j, 513 + j),
                                cw[:, oc * 4 + j:oc * 4 + j + 1], acc,
                                ALU.mult, ALU.add))(),
                                reads=[bufs[oc], CONST], writes=[acb])
                    sop((lambda oc=oc, base=base, acc=acc:
                         lambda e: e.activation(
                        qkv_ap(base, oc, 4, 516), acc, AF.Silu,
                        bias=0.0, scale=1.0))(),
                        reads=[acb], writes=[bufs[oc]])
            for hc in range(NKC):
                sop((lambda hc=hc: lambda e: e.activation(
                    A(0, 128, SQ5_o, ST), qkv_ap(KO, hc, 4, 516),
                    AF.Square, bias=0.0, scale=1.0))(),
                    reads=[bK[hc]],
                    writes=[B["sq512"], B["ps0"], B["ps1"]])
                mm(pb[4][0:NH, 0:ST], c_hmask[:, hc * NH:(hc + 1) * NH],
                   A(0, 128, SQ5_o, ST), [CONST, B["sq512"]], bPB[4],
                   hc == 0, hc == NKC - 1)
            sop(lambda e: e.activation(lk16, pb[4][0:NH, 0:ST], AF.Ln,
                                       bias=c_eps6[:], scale=1.0),
                reads=[bPB[4], CONST], writes=[B["lk16"]])
            vop(lambda e: e.tensor_scalar_mul(lk16, lk16, -0.5),
                reads=[], writes=[B["lk16"]])
            if is_main:
                for hc in range(NKC):
                    sop((lambda hc=hc: lambda e: e.activation(
                        A(0, 128, SQ5_o, ST), qkv_ap(QO, hc, 4, 516),
                        AF.Square, bias=0.0, scale=1.0))(),
                        reads=[bQ[hc]],
                        writes=[B["sq512"], B["ps0"], B["ps1"]])
                    mm(pb[5][0:NH, 0:ST],
                       c_hmask[:, hc * NH:(hc + 1) * NH],
                       A(0, 128, SQ5_o, ST), [CONST, B["sq512"]],
                       bPB[5], hc == 0, hc == NKC - 1)
                sop(lambda e: e.activation(rq16, pb[5][0:NH, 0:ST],
                                           AF.Sqrt, bias=c_eps6[:],
                                           scale=1.0),
                    reads=[bPB[5], CONST], writes=[B["rq16"]])
                vop(lambda e: e.reciprocal(rq16, rq16),
                    reads=[], writes=[B["rq16"]])
                vop(lambda e: e.tensor_scalar_mul(rq16, rq16, 0.125),
                    reads=[], writes=[B["rq16"]])
            vop(lambda e: e.tensor_tensor(RA16, lk16, lnb16, ALU.add),
                reads=[B["lk16"], B["lnb16"]], writes=[B["RA16"]])
            vop(lambda e: e.tensor_tensor(RA16, RA16, G16, ALU.add),
                reads=[B["G16"]], writes=[B["RA16"]])
            vop(lambda e: e.tensor_tensor(RB16, lk16, G16, ALU.subtract),
                reads=[B["lk16"], B["G16"]], writes=[B["RB16"]])
            if is_main:
                for oc in range(NKC):
                    slot, sb_, qn = next_up()
                    ld(qn, slot, sb_, dr["wgt"][oc], H)
                    for kc in range(NKC):
                        mm(pb[oc][:, 0:ST],
                           slot[:, kc * 128:(kc + 1) * 128],
                           xln_ap(kc, 0, ST), [sb_, bXLN[kc]], bPB[oc],
                           kc == 0, kc == NKC - 1)
                for oc in range(NKC):
                    sop((lambda oc=oc: lambda e: e.activation(
                        xln_ap(oc, 0, ST), pb[oc][:, 0:ST], AF.Silu,
                        bias=0.0, scale=1.0))(),
                        reads=[bPB[oc]], writes=[bXLN[oc]])
            nch = NCH if lim_chunks is None else min(NCH, lim_chunks)
            if not skip_scan:
                for c in range(nch):
                    chunk(t0, c * C, is_main)

        for sp in range(n_pre):
            supertile(sp, False, qproj=(sp == n_pre - 1))
        for sm_i in range(n_main):
            supertile(sm_i, True, qproj=True)

        if skip_ffn:
            st_toks = []
            if dbg:
                allb = [b for b in B.values()]
                dumps = [
                    (yout[0:128, 0:128], A(0, 128, Bm_o, C)),
                    (yout[0:128, 128:256], A(0, 128, E_o, C)),
                    (yout[0:128, 256:384], A(0, 128, MTm_o, C)),
                    (yout[128:130, 0:512], rfp[0:2, 0:512]),
                    (yout[132:134, 0:512], rfp[32:34, 0:512]),
                    (yout[162:164, 0:512], rfp[0:2, 2048:2560]),
                    (yout[166:168, 0:512], rfp[32:34, 2048:2560]),
                    (yout[196:324, 0:144], tokcol[:]),
                    (yout[324:452, 0:128], stackA[:]),
                    (yout[452:468, 0:512], sml[:, 2560:2560 + 512]),
                    (yout[468:484, 0:512], sml[:, 3072:3072 + 512]),
                    (yout[484:500, 0:512], sml[:, 2048:2048 + 512]),
                    (yout[500:516, 0:512], beta16),
                    (yout[516:644, 0:512], arena[:, KO:KO + 512]),
                    (yout[644:772, 0:512], arena[:, QO:QO + 512]),
                ]
                for (dst, src) in dumps:
                    st_toks.append(em.dma("Y", "qout",
                                          (lambda d=dst, sr=src:
                                           lambda e: e.dma_start(
                        out=d, in_=sr))(), reads=allb))
            else:
                for kc in range(NKC):
                    st_toks.append(em.dma("Y", "qout",
                                          (lambda kc=kc: lambda e: e.dma_start(
                        out=yout[kc * 128:(kc + 1) * 128, :],
                        in_=xres_ap(kc, 0, TM)))(), reads=[bXR[kc]]))
            em.op("Y", lambda e: e.nop(), extra=st_toks)
            return _emit(nc, em, ctx)
        attn_only = ([bQ[k] for k in range(NKC)]
                     + [bK[k] for k in range(NKC)]
                     + [bV[k] for k in range(NKC)] + bXLN + bOFM
                     + [B[n] for n in ("ub", "Ub", "tokcol", "eB", "eBT",
                                       "eM", "Bm", "BTm", "MTm", "wkT",
                                       "khat", "Otok", "sqd", "sq512",
                                       "cacc0", "cacc1", "msr", "S",
                                       "stackA", "stackB", "lamrep",
                                       "lamc", "lam1", "e116", "rfp",
                                       "g16", "beta16", "lnb16", "lk16",
                                       "G16", "RA16", "RB16", "rq16",
                                       "t16a", "t16b", "xstage", "hist",
                                       "e2b", "lamb", "ub2", "khat2",
                                       "MTm2")]
                     + [bPk[k] for k in range(6)]
                     + [bPTk[k] for k in range(6)])
        em.op("V", lambda e: e.engine_nop(), reads=attn_only,
              writes=(bX2LN + bH + [bDN[0], bDN[1]]
                      + [B[n] for n in ("hsil", "cwst", "cwrow",
                                        "logit8", "cwm", "cwmask",
                                        "cwsig")]))
        for tt in range(NTT):
            t0 = tt * ST
            ln_pass(lambda kc, t0=t0: xres_ap(kc, t0, t0 + ST), bXR,
                    lambda kc, t0=t0: x2ln_ap(kc, t0, t0 + ST), bX2LN)
            for kc in range(NKC):
                mm(pb[0][0:NE, 0:ST], c_wrt[:, kc * NE:(kc + 1) * NE],
                   x2ln_ap(kc, t0, t0 + ST), [CONST, bX2LN[kc]], bPB[0],
                   kc == 0, kc == NKC - 1)
            vop(lambda e: e.tensor_scalar(logit8, pb[0][0:NE, 0:ST],
                                          c_rb8[:], None, ALU.add),
                reads=[bPB[0], CONST], writes=[B["logit8"]])
            for s4 in range(4):
                mm(pb[1][:, s4 * NE:(s4 + 1) * NE],
                   logit8[:, s4 * C:(s4 + 1) * C], c_id[0:NE, 0:NE],
                   [B["logit8"], CONST], bPB[1], s4 == 0, s4 == 3,
                   tr=True)
            for s4 in range(4):
                col = s4 * NE
                vop((lambda col=col: lambda e: e.tensor_copy(
                    stackA[:, 0:NE], pb[1][:, col:col + NE]))(),
                    reads=[bPB[1]], writes=[B["cwsig"]])
                vop(lambda e: e.reduce_max(stackA[:, 16:17],
                                           stackA[:, 0:NE], axis=AX.X),
                    reads=[B["cwsig"]], writes=[B["cwm"]])
                vop(lambda e: e.tensor_scalar(
                    stackA[:, 24:24 + NE], stackA[:, 0:NE],
                    stackA[:, 16:17], None, ALU.is_ge),
                    reads=[B["cwsig"], B["cwm"]], writes=[B["cwmask"]])
                vop(lambda e: e.scalar_tensor_tensor(
                    stackA[:, 24:24 + NE], stackA[:, 24:24 + NE],
                    NEGBIG, stackA[:, 0:NE], ALU.mult, ALU.add),
                    reads=[B["cwsig"]], writes=[B["cwmask"]])
                vop(lambda e: e.reduce_max(stackA[:, 16:17],
                                           stackA[:, 24:24 + NE],
                                           axis=AX.X),
                    reads=[B["cwmask"]], writes=[B["cwm"]])
                sop(lambda e: e.activation(stackA[:, 32:32 + NE],
                                           stackA[:, 0:NE], AF.Sigmoid,
                                           bias=0.0, scale=1.0),
                    reads=[B["cwsig"]], writes=[B["cwsig"]])
                vop(lambda e: e.tensor_scalar(
                    stackA[:, 24:24 + NE], stackA[:, 0:NE],
                    stackA[:, 16:17], None, ALU.is_ge),
                    reads=[B["cwsig"], B["cwm"]], writes=[B["cwmask"]])
                vop(lambda e: e.tensor_tensor(
                    stackA[:, 40:40 + NE], stackA[:, 24:24 + NE],
                    stackA[:, 32:32 + NE], ALU.mult),
                    reads=[B["cwmask"], B["cwsig"]], writes=[B["cwsig"]])
                mm(pb[2][0:NE, s4 * C:(s4 + 1) * C],
                   stackA[:, 40:40 + NE], c_id[:], [B["cwsig"], CONST],
                   bPB[2], s4 == 0, s4 == 3, tr=True)
            vop((lambda t0=t0: lambda e: e.tensor_copy(
                cwst[:, t0:t0 + ST], pb[2][0:NE, 0:ST]))(),
                reads=[bPB[2]], writes=[B["cwst"]])

        for e_i in range(-1, NE):
            for tt in range(NTT):
                t0 = tt * ST
                if e_i >= 0:
                    em.dma("G", "qcw", (lambda e_i=e_i, t0=t0:
                                        lambda e: e.dma_start(
                        out=stp[0:1, 0:ST],
                        in_=cwst[e_i:e_i + 1, t0:t0 + ST]))(),
                        reads=[B["cwst"]],
                        writes=[B["cwrow"], B["statr"]])
                    mm(pb[3][:, 0:ST], c_onesrow[0:1, :], stp[0:1, 0:ST],
                       [B["cwrow"]], bPB[3], True, True)
                for half in range(2):
                    for oc8 in range(8):
                        oc = half * 8 + oc8
                        if e_i < 0:
                            dg = dr["wgs"][oc]; du = dr["wus"][oc]
                        else:
                            dg = dr["wge"][e_i * NIC + oc]
                            du = dr["wue"][e_i * NIC + oc]
                        slg, sbg, qg = next_up()
                        ld(qg, slg, sbg, dg, H)
                        slu, sbu, qu = next_up()
                        ld(qu, slu, sbu, du, H)
                        for kc in range(NKC):
                            mm(pb[0][:, 0:ST],
                               slg[:, kc * 128:(kc + 1) * 128],
                               x2ln_ap(kc, t0, t0 + ST),
                               [sbg, bX2LN[kc]], bPB[0], kc == 0,
                               kc == NKC - 1)
                        for kc in range(NKC):
                            mm(pb[1][:, 0:ST],
                               slu[:, kc * 128:(kc + 1) * 128],
                               x2ln_ap(kc, t0, t0 + ST),
                               [sbu, bX2LN[kc]], bPB[1], kc == 0,
                               kc == NKC - 1)
                        sop(lambda e: e.activation(
                            A(0, 128, HS_o, ST), pb[0][:, 0:ST],
                            AF.Silu, bias=0.0, scale=1.0),
                            reads=[bPB[0]], writes=[B["hsil"]])
                        vop((lambda oc8=oc8:
                             lambda e: e.scalar_tensor_tensor(
                            h_ap(oc8, 0, ST), pb[1][:, 0:ST], 1.0,
                            A(0, 128, HS_o, ST), ALU.mult, ALU.mult))(),
                            reads=[bPB[1], B["hsil"]], writes=[bH[oc8]])
                        if e_i >= 0:
                            vop((lambda oc8=oc8:
                                 lambda e: e.tensor_tensor(
                                h_ap(oc8, 0, ST), h_ap(oc8, 0, ST),
                                pb[3][:, 0:ST], ALU.mult))(),
                                reads=[bPB[3]], writes=[bH[oc8]])
                    for hcb in range(NKC):
                        if e_i < 0:
                            dd = dr["wds"][hcb]
                        else:
                            dd = dr["wde"][e_i * NKC + hcb]
                        sld, sbd, qd = next_dn()
                        ld(qd, sld, sbd,
                           dd[:, half * 1024:(half + 1) * 1024], 1024)
                        bank = bPB[4 + (hcb % 2)]
                        pbank = pb[4 + (hcb % 2)]
                        for oc8 in range(8):
                            mm(pbank[:, 0:ST],
                               sld[:, oc8 * 128:(oc8 + 1) * 128],
                               h_ap(oc8, 0, ST), [sbd, bH[oc8]], bank,
                               oc8 == 0, oc8 == 7)
                        vop((lambda hcb=hcb, t0=t0, pbank=pbank:
                             lambda e: e.tensor_tensor(
                            xres_ap(hcb, t0, t0 + ST),
                            xres_ap(hcb, t0, t0 + ST),
                            pbank[:, 0:ST], ALU.add))(),
                            reads=[bank], writes=[bXR[hcb]])

        st_toks = []
        for kc in range(NKC):
            st_toks.append(em.dma("Y", "qout",
                                  (lambda kc=kc: lambda e: e.dma_start(
                out=yout[kc * 128:(kc + 1) * 128, :],
                in_=xres_ap(kc, 0, TM)))(), reads=[bXR[kc]]))
        em.op("Y", lambda e: e.nop(), extra=st_toks)

        return _emit(nc, em, ctx)


_BUILD_CACHE = {}


def _prep_static(inputs):
    w = {}
    w["wq"] = _tile_lhsT(inputs["L0_attn_Wq"])
    w["wk"] = _tile_lhsT(inputs["L0_attn_Wk"])
    w["wv"] = _tile_lhsT(inputs["L0_attn_Wv"])
    w["wgt"] = _tile_lhsT(inputs["L0_attn_Wg"])
    w["wo"] = _tile_lhsT(inputs["L0_attn_Wo"])
    w["wgs"] = _tile_lhsT(inputs["L0_ffn_Wg_s"])
    w["wus"] = _tile_lhsT(inputs["L0_ffn_Wu_s"])
    w["wds"] = _tile_lhsT(inputs["L0_ffn_Wd_s"])
    w["wge"] = np.concatenate([_tile_lhsT(inputs["L0_ffn_Wg"][e])
                               for e in range(NE)], 0)
    w["wue"] = np.concatenate([_tile_lhsT(inputs["L0_ffn_Wu"][e])
                               for e in range(NE)], 0)
    w["wde"] = np.concatenate([_tile_lhsT(inputs["L0_ffn_Wd"][e])
                               for e in range(NE)], 0)
    w["wbt"] = _tile_small(inputs["L0_attn_Wb"])
    w["wat"] = _tile_small(inputs["L0_attn_Wa"])
    w["wrt"] = _tile_small(inputs["L0_ffn_Wr"])
    for nm, src in (("convq", "L0_attn_conv_q"),
                    ("convk", "L0_attn_conv_k"),
                    ("convv", "L0_attn_conv_v")):
        cv = inputs[src]
        w[nm] = np.ascontiguousarray(
            cv.reshape(NKC, 128, 4).transpose(1, 0, 2).reshape(128,
                                                               NKC * 4))
    w["alog"] = np.ascontiguousarray(inputs["L0_attn_A_log"].reshape(NH, 1))
    w["dtb"] = np.ascontiguousarray(
        inputs["L0_attn_dt_bias"].reshape(NH, 1))
    w["normw2"] = np.ascontiguousarray(
        np.tile(inputs["L0_attn_norm_w"], 2).reshape(128, 1))
    w["rb8"] = np.ascontiguousarray(inputs["L0_ffn_rb"].reshape(NE, 1))
    w["ident"] = np.eye(128, dtype=np.float32)
    import numpy as _np
    lo, up, upi = _masks()
    w["mb_lo"] = _np.ascontiguousarray(_np.tile(lo, (1, 2)))
    w["mb_upi"] = _np.ascontiguousarray(_np.tile(upi, (1, 2)))
    w["hmask"] = _headmask()
    return w


def kernel(**inputs):
    inputs = {k: np.asarray(v, dtype=np.float32) for k, v in inputs.items()}
    x = inputs["L0_attn_hidden"]
    Bn, T, _ = x.shape
    halfT = T // 2
    w = _prep_static(inputs)
    if "full" not in _BUILD_CACHE:
        _BUILD_CACHE["full"] = build_nc(4, 4)
    nc = _BUILD_CACHE["full"]
    in_maps = []
    for core in range(8):
        b, hf = core // 2, core % 2
        m = dict(w)
        m["xmain"] = np.ascontiguousarray(
            x[b, hf * halfT:(hf + 1) * halfT].T)
        m["xpre"] = (np.zeros((H, halfT), np.float32) if hf == 0
                     else np.ascontiguousarray(x[b, 0:halfT].T))
        in_maps.append(m)
    res = run_bass_kernel_spmd(nc, in_maps, core_ids=list(range(8)))
    out = np.empty_like(x)
    for core in range(8):
        b, hf = core // 2, core % 2
        out[b, hf * halfT:(hf + 1) * halfT] = res.results[core]["y"].T
    return out
